# version 13
# baseline (speedup 1.0000x reference)
import numpy as np
import ml_dtypes
from contextlib import ExitStack
import concourse.bass as bass
import concourse.mybir as mybir
from concourse.bass_utils import run_bass_kernel_spmd

F32 = mybir.dt.float32
BF16 = mybir.dt.bfloat16
AF = mybir.ActivationFunctionType
ALU = mybir.AluOpType

T = 4096
D = 1024
TN = 32
TCN = 8
DC = 8
LC = 4
NP = 8
IM = 938
ICW = [128] * 7 + [42]
EPS = 1e-5

ENGS = ("pe", "act", "dve", "pool", "sp")
SERIAL = False
SYNC_SAME_ENGINE = False
SYNC_ENGINES = {"dve", "act", "pool"}

LAUNCH_PLAN = [["a0", "f0", "a1", "f1"]]
DEBUG_A0_SEGS = 5
DEBUG_EPI = 9


class Prog:

    def __init__(self, nc, es):
        self.nc = nc
        self.es = es
        self.sem_h = {}
        self.count = {}
        self.observed = {e: {} for e in ENGS}
        self.cur = None
        self.n_waits = 0
        self.n_instr = 0

    def sem(self, name):
        if name not in self.sem_h:
            self.sem_h[name] = self.es.enter_context(self.nc.semaphore(name))
            self.count[name] = 0
        return name

    def begin(self):
        assert self.cur is None
        self.cur = []
        self.lastw = {}
        self.readers = {}
        self.group_prev = {}
        self.snapshot = dict(self.count)

    def emit(self, eng, fn, reads=(), writes=(), dma=None):
        ps = [k for k in reads if isinstance(k, tuple) and k
              and isinstance(k[0], str) and k[0].startswith("ps")]
        if ps:
            reads = [k for k in reads if k not in ps]
            writes = list(writes) + ps
        i = len(self.cur)
        deps = set()
        for k in reads:
            w = self.lastw.get(k)
            if w is not None:
                deps.add(w)
        for k in writes:
            w = self.lastw.get(k)
            if w is not None:
                deps.add(w)
            deps.update(self.readers.get(k, ()))
        if dma is not None:
            pidx = self.group_prev.get(dma)
            if pidx is not None:
                deps.add(pidx)
            self.group_prev[dma] = i
        if SERIAL and i > 0:
            deps.add(i - 1)
        deps.discard(i)
        self.cur.append(dict(eng=eng, fn=fn, dma=dma, deps=deps,
                             mark=False, sem=None, val=None, waits=()))
        for k in reads:
            self.readers.setdefault(k, []).append(i)
        for k in writes:
            self.lastw[k] = i
            self.readers[k] = []
        return i

    @staticmethod
    def _nw(p, c):
        if SERIAL or SYNC_SAME_ENGINE:
            return True
        if p["eng"] == c["eng"] and p["eng"] in SYNC_ENGINES:
            return True
        return (p["dma"] is not None or p["eng"] != c["eng"]
                or c["dma"] is not None)

    def end(self):
        cur = self.cur
        self.n_instr += len(cur)
        for c in cur:
            for d in c["deps"]:
                if self._nw(cur[d], c):
                    cur[d]["mark"] = True
        lasts = {}
        for rec in cur:
            if rec["dma"] is None and rec["fn"] is not None:
                lasts[rec["eng"]] = rec
        for rec in lasts.values():
            rec["mark"] = True
        for rec in cur:
            if rec["dma"] is not None:
                s = self.sem("d_" + rec["dma"])
                self.count[s] += 16
                rec["sem"], rec["val"] = s, self.count[s]
            elif rec["mark"]:
                s = self.sem("e_" + rec["eng"])
                self.count[s] += 1
                rec["sem"], rec["val"] = s, self.count[s]
        streams = {e: [] for e in ENGS}
        for rec in cur:
            streams[rec["eng"]].append(rec)
        for e, lst in streams.items():
            obs = self.observed[e]
            first = True
            for rec in lst:
                waits = {}
                if first:
                    for s, v in self.snapshot.items():
                        if v > obs.get(s, 0):
                            waits[s] = v
                    first = False
                for d in rec["deps"]:
                    p = cur[d]
                    if not self._nw(p, rec):
                        continue
                    assert p["sem"] is not None
                    if p["val"] > obs.get(p["sem"], 0) and \
                       p["val"] > waits.get(p["sem"], 0):
                        waits[p["sem"]] = p["val"]
                rec["waits"] = sorted(waits.items())
                for s, v in waits.items():
                    obs[s] = max(obs.get(s, 0), v)
                self.n_waits += len(rec["waits"])
        sem_h = self.sem_h
        with self.nc.Block() as block:
            def mk(e):
                def run(h):
                    for rec in streams[e]:
                        for s, v in rec["waits"]:
                            h.wait_ge(sem_h[s], v)
                        if rec["fn"] is None:
                            continue
                        b = rec["fn"](h)
                        if rec["sem"] is not None:
                            b.then_inc(sem_h[rec["sem"]],
                                       16 if rec["dma"] is not None else 1)
                return run
            if streams["sp"]:
                block.sync(mk("sp"))
            if streams["pool"]:
                block.gpsimd(mk("pool"))
            if streams["dve"]:
                block.vector(mk("dve"))
            if streams["act"]:
                block.scalar(mk("act"))
            if streams["pe"]:
                block.tensor(mk("pe"))
        self.cur = None

    def drain(self):
        self.begin()
        self.emit("sp", None)
        self.end()


class Build:
    def __init__(self, nc, p, es):
        self.nc = nc
        self.p = p
        self.es = es
        self.t = {}

    def _u(self, name):
        self._nid = getattr(self, "_nid", 0) + 1
        return "%s_%d" % (name, self._nid)

    def sb(self, name, shape, dtype):
        h = self.es.enter_context(self.nc.sbuf_tensor(name, list(shape), dtype))
        self.t[name] = h
        return h

    def alloc_common(self):
        sb = self.sb
        sb("u_fm", [128, DC, T], BF16)
        sb("xstage", [128, 4, D], F32)
        sb("xold", [128, 2, D], F32)
        sb("xnew", [128, 2, D], F32)
        sb("tnorm", [128, D], F32)
        sb("u_tm", [128, 2, D], BF16)
        self.trash = None
        sb("stats", [128, 64], F32)
        sb("estats", [128, 32], F32)
        sb("ident", [128, 128], BF16)
        sb("ones_b", [128, 128], BF16)

    def setup_common(self):
        p, t = self.p, self.t
        p.emit("dve", lambda h: h.memset(t["ones_b"][:], 1.0), writes=[("ones_b",)])
        p.emit("pool", lambda h: h.affine_select(
            out=t["ident"][:], in_=t["ones_b"][:], pattern=[[-1, 128]],
            compare_op=ALU.is_equal, fill=0.0, base=0, channel_multiplier=1),
            reads=[("ones_b",)], writes=[("ident",)])

    def ln_group(self, x_ap, tt0, n, dest_fn, ps_tr, xin_name):
        p, t = self.p, self.t
        xst, st, utm = t["xstage"], t["stats"], t["u_tm"]
        gp = (tt0 // 4) % 2
        so = gp * 32
        kb = ("lnst", gp)
        for i in range(n):
            t0 = (tt0 + i) * 128
            p.emit("sp", lambda h, i=i, t0=t0: h.dma_start(
                out=xst[:, i, :], in_=x_ap[t0:t0 + 128, :]),
                reads=[(xin_name, tt0 + i)], writes=[("xstage", i)],
                dma="xs%d" % (i % 2))
            p.emit("dve", lambda h, i=i: h.tensor_scalar(
                out=xst[:, i, :], in0=xst[:, i, :], scalar1=0.0, scalar2=None,
                op0=ALU.add, op1=ALU.add, accum_out=st[:, so + i:so + i + 1]),
                reads=[("xstage", i)], writes=[kb + ("s", i), ("xstage", i)])
            p.emit("act", lambda h, i=i: h.activation(
                out=self.trash(D), in_=xst[:, i, :], func=AF.Square,
                accum_out=st[:, so + 4 + i:so + 5 + i]),
                reads=[("xstage", i)], writes=[kb + ("q", i)])
        skeys = [kb + ("s", i) for i in range(n)] + \
                [kb + ("q", i) for i in range(n)]
        p.emit("dve", lambda h: h.tensor_scalar(
            out=st[:, so + 8:so + 8 + n], in0=st[:, so:so + n], scalar1=1.0 / D,
            scalar2=None, op0=ALU.mult, op1=ALU.bypass),
            reads=skeys, writes=[kb + ("mean",)])
        p.emit("dve", lambda h: h.tensor_scalar(
            out=st[:, so + 12:so + 12 + n], in0=st[:, so + 4:so + 4 + n],
            scalar1=1.0 / D, scalar2=EPS, op0=ALU.mult, op1=ALU.add),
            reads=skeys, writes=[kb + ("v1",)])
        p.emit("dve", lambda h: h.tensor_tensor(
            out=st[:, so + 16:so + 16 + n], in0=st[:, so + 8:so + 8 + n],
            in1=st[:, so + 8:so + 8 + n], op=ALU.mult),
            reads=[kb + ("mean",)], writes=[kb + ("m2",)])
        p.emit("dve", lambda h: h.tensor_tensor(
            out=st[:, so + 20:so + 20 + n], in0=st[:, so + 12:so + 12 + n],
            in1=st[:, so + 16:so + 16 + n], op=ALU.subtract),
            reads=[kb + ("v1",), kb + ("m2",)], writes=[kb + ("var",)])
        p.emit("act", lambda h: h.activation(
            out=st[:, so + 24:so + 24 + n], in_=st[:, so + 20:so + 20 + n],
            func=AF.Sqrt),
            reads=[kb + ("var",)], writes=[kb + ("std",)])
        p.emit("dve", lambda h: h.reciprocal(
            st[:, so + 28:so + 28 + n], st[:, so + 24:so + 24 + n]),
            reads=[kb + ("std",)], writes=[kb + ("rstd",)])
        for i in range(n):
            us = i % 2
            p.emit("dve", lambda h, i=i, us=us: h.tensor_scalar(
                out=utm[:, us, :], in0=xst[:, i, :],
                scalar1=st[:, so + 8 + i:so + 9 + i],
                scalar2=st[:, so + 28 + i:so + 29 + i],
                op0=ALU.subtract, op1=ALU.mult),
                reads=[("xstage", i), kb + ("mean",), kb + ("rstd",)],
                writes=[("u_tm", us)])
            for j in range(DC):
                pj = ps_tr[j % 2]
                p.emit("pe", lambda h, us=us, j=j, pj=pj: h.transpose(
                    pj[:], utm[:, us, j * 128:(j + 1) * 128], t["ident"][:]),
                    reads=[("u_tm", us), ("ident",)], writes=[("ps_tr", j % 2)])
                dap, dkey = dest_fn(i, j)
                if j % 2 == 0:
                    p.emit("dve", lambda h, dap=dap, pj=pj: h.tensor_copy(
                        dap, pj[:]),
                        reads=[("ps_tr", j % 2)], writes=[dkey])
                else:
                    p.emit("act", lambda h, dap=dap, pj=pj: h.activation(
                        out=dap, in_=pj[:], func=AF.Copy),
                        reads=[("ps_tr", j % 2)], writes=[dkey])

    def _stat_tail(self, st, so, kb):
        p = self.p
        p.emit("dve", lambda h: h.tensor_scalar(
            out=st[:, so + 2:so + 3], in0=st[:, so:so + 1], scalar1=1.0 / D,
            scalar2=None, op0=ALU.mult, op1=ALU.bypass),
            reads=[kb + (0,)], writes=[kb + (2,)])
        p.emit("dve", lambda h: h.tensor_scalar(
            out=st[:, so + 3:so + 4], in0=st[:, so + 1:so + 2], scalar1=1.0 / D,
            scalar2=EPS, op0=ALU.mult, op1=ALU.add),
            reads=[kb + (1,)], writes=[kb + (3,)])
        p.emit("dve", lambda h: h.tensor_tensor(
            out=st[:, so + 4:so + 5], in0=st[:, so + 2:so + 3],
            in1=st[:, so + 2:so + 3], op=ALU.mult),
            reads=[kb + (2,)], writes=[kb + (4,)])
        p.emit("dve", lambda h: h.tensor_tensor(
            out=st[:, so + 5:so + 6], in0=st[:, so + 3:so + 4],
            in1=st[:, so + 4:so + 5], op=ALU.subtract),
            reads=[kb + (3,), kb + (4,)], writes=[kb + (5,)])
        p.emit("act", lambda h: h.activation(
            out=st[:, so + 6:so + 7], in_=st[:, so + 5:so + 6], func=AF.Sqrt),
            reads=[kb + (5,)], writes=[kb + (6,)])
        p.emit("dve", lambda h: h.reciprocal(
            st[:, so + 7:so + 8], st[:, so + 6:so + 7]),
            reads=[kb + (6,)], writes=[kb + (7,)])

    def epilogue_tiles(self, tiles, x_in_ap, x_out_ap, xin_name, xout_name):
        p, t = self.p, self.t
        st = t["estats"]
        n = len(tiles)
        bp = (tiles[0][0] // n) % 2 if n else 0
        so = bp * 16
        kb = ("est", bp)
        xo, tn, xn = t["xold"], t["tnorm"], t["xnew"]
        pkeys = []
        for k, (tt, o_parts) in enumerate(tiles):
            t0 = tt * 128
            p.emit("sp", lambda h, k=k, t0=t0: h.dma_start(
                out=xo[:, k, :], in_=x_in_ap[t0:t0 + 128, :]),
                reads=[(xin_name, tt)], writes=[("xold", k)], dma="xo%d" % k)
            for i, (ap, w, key) in enumerate(o_parts):
                if len(o_parts) == 1:
                    sdst = st[:, so + k:so + k + 1]
                    qdst = st[:, so + 2 + k:so + 3 + k]
                else:
                    sdst = st[:, so + 8 + 2 * i + k:so + 9 + 2 * i + k]
                    qdst = st[:, so + 12 + 2 * i + k:so + 13 + 2 * i + k]
                p.emit("dve", lambda h, ap=ap, sdst=sdst: h.tensor_scalar(
                    out=ap, in0=ap, scalar1=0.0, scalar2=None,
                    op0=ALU.add, op1=ALU.add, accum_out=sdst),
                    reads=[key], writes=[kb + ("ps", k, i), key])
                p.emit("act", lambda h, ap=ap, w=w, qdst=qdst: h.activation(
                    out=self.trash(w), in_=ap, func=AF.Square,
                    accum_out=qdst),
                    reads=[key], writes=[kb + ("pq", k, i)])
                pkeys += [kb + ("ps", k, i), kb + ("pq", k, i)]
        if len(tiles[0][1]) == 2:
            p.emit("dve", lambda h: h.tensor_tensor(
                out=st[:, so:so + n], in0=st[:, so + 8:so + 8 + n],
                in1=st[:, so + 10:so + 10 + n], op=ALU.add),
                reads=pkeys, writes=[kb + ("s",)])
            p.emit("dve", lambda h: h.tensor_tensor(
                out=st[:, so + 2:so + 2 + n], in0=st[:, so + 12:so + 12 + n],
                in1=st[:, so + 14:so + 14 + n], op=ALU.add),
                reads=pkeys, writes=[kb + ("q",)])
        skeys = pkeys + [kb + ("s",), kb + ("q",)]
        p.emit("dve", lambda h: h.tensor_scalar(
            out=st[:, so + 4:so + 4 + n], in0=st[:, so:so + n],
            scalar1=1.0 / D, scalar2=None, op0=ALU.mult, op1=ALU.bypass),
            reads=skeys, writes=[kb + ("mean",)])
        p.emit("dve", lambda h: h.tensor_scalar(
            out=st[:, so + 6:so + 6 + n], in0=st[:, so + 2:so + 2 + n],
            scalar1=1.0 / D, scalar2=EPS, op0=ALU.mult, op1=ALU.add),
            reads=skeys, writes=[kb + ("v1",)])
        p.emit("dve", lambda h: h.tensor_tensor(
            out=st[:, so + 8:so + 8 + n], in0=st[:, so + 4:so + 4 + n],
            in1=st[:, so + 4:so + 4 + n], op=ALU.mult),
            reads=[kb + ("mean",)], writes=[kb + ("m2",)])
        p.emit("dve", lambda h: h.tensor_tensor(
            out=st[:, so + 10:so + 10 + n], in0=st[:, so + 6:so + 6 + n],
            in1=st[:, so + 8:so + 8 + n], op=ALU.subtract),
            reads=[kb + ("v1",), kb + ("m2",)], writes=[kb + ("var",)])
        p.emit("act", lambda h: h.activation(
            out=st[:, so + 12:so + 12 + n], in_=st[:, so + 10:so + 10 + n],
            func=AF.Sqrt),
            reads=[kb + ("var",)], writes=[kb + ("std",)])
        p.emit("dve", lambda h: h.reciprocal(
            st[:, so + 14:so + 14 + n], st[:, so + 12:so + 12 + n]),
            reads=[kb + ("std",)], writes=[kb + ("rstd",)])
        for k, (tt, o_parts) in enumerate(tiles):
            t0 = tt * 128
            off = 0
            tkeys = []
            for ap, w, key in o_parts:
                p.emit("dve", lambda h, ap=ap, off=off, w=w, k=k: h.tensor_scalar(
                    out=tn[:, off:off + w], in0=ap,
                    scalar1=st[:, so + 4 + k:so + 5 + k],
                    scalar2=st[:, so + 14 + k:so + 15 + k],
                    op0=ALU.subtract, op1=ALU.mult),
                    reads=[key, kb + ("mean",), kb + ("rstd",)],
                    writes=[("tnorm", off)])
                tkeys.append(("tnorm", off))
                off += w
            p.emit("dve", lambda h, k=k: h.tensor_tensor(
                out=xn[:, k, :], in0=xo[:, k, :], in1=tn[:, 0:D], op=ALU.add),
                reads=[("xold", k)] + tkeys, writes=[("xnew", k)])
            p.emit("sp", lambda h, k=k, t0=t0: h.dma_start(
                out=x_out_ap[t0:t0 + 128, :], in_=xn[:, k, :]),
                reads=[("xnew", k)], writes=[(xout_name, tt)], dma="xn%d" % k)

    def epilogue_tile(self, o_parts, x_in_ap, x_out_ap, tt, xin_name, xout_name):
        self.epilogue_tiles([(tt, o_parts)], x_in_ap, x_out_ap,
                            xin_name, xout_name)

    def attn_pairs(self, cfg):
        p, t = self.p, self.t
        ps_s, ps_pv, ps_u = cfg["ps_s"], cfg["ps_pv"], cfg["ps_u"]
        pbuf, qbuf = cfg["pbuf"], cfg["qbuf"]
        recrow, recrow_bf = cfg["recrow"], cfg["recrow_bf"]
        recbc, ybst = cfg["recbc"], cfg["ybst"]
        yfm = cfg["yfm"]
        LAG = cfg.get("lag", 3)
        npb = pbuf.shape[1]

        def qslot(pair, tc):
            return tc % 2

        def qproj_emit(pair, tc):
            par = qslot(pair, tc)
            for d in range(cfg["qkc"]):
                wap, wkey = cfg["qw"](pair, d)
                sap, skey = cfg["q_src"](d, tc)
                p.emit("pe", lambda h, wap=wap, sap=sap, d=d: h.matmul(
                    ps_u[0][:], wap, sap,
                    start=(d == 0), stop=(d == cfg["qkc"] - 1)),
                    reads=[wkey, skey], writes=[("ps_u", 0)])
            p.emit("dve", lambda h, par=par: h.tensor_copy(
                qbuf[:, par, :], ps_u[0][:]),
                reads=[("ps_u", 0)], writes=[("qbuf", par)])

        def epi_emit(pair, tc):
            t0 = tc * 512
            par = tc % 2
            for head in range(2):
                p.emit("act", lambda h, head=head: h.activation(
                    out=recrow, in_=ps_pv[head][64:65, :],
                    func=AF.Ln),
                    reads=[("ps_pv", head)], writes=[("recrow",)])
                p.emit("act", lambda h: h.activation(
                    out=recrow_bf, in_=recrow,
                    func=AF.Exp, scale=-1.0),
                    reads=[("recrow",)], writes=[("recrow_bf",)])
                p.emit("pe", lambda h: h.matmul(
                    ps_u[1][0:64, :], t["ones_b"][64:65, 0:64],
                    recrow_bf, start=True, stop=True),
                    reads=[("recrow_bf",), ("ones_b",)], writes=[("ps_u", 1)])
                p.emit("dve", lambda h: h.tensor_copy(
                    recbc, ps_u[1][0:64, :]),
                    reads=[("ps_u", 1)], writes=[("recbc",)])
                if head == 0:
                    p.emit("dve", lambda h, pair=pair, t0=t0: h.tensor_tensor(
                        out=yfm[0:64, pair, t0:t0 + 512],
                        in0=ps_pv[0][0:64, :], in1=recbc, op=ALU.mult),
                        reads=[("ps_pv", 0), ("recbc",)],
                        writes=[("yfm", pair, tc, 0)])
                else:
                    p.emit("dve", lambda h: h.tensor_tensor(
                        out=ybst, in0=ps_pv[1][0:64, :],
                        in1=recbc, op=ALU.mult),
                        reads=[("ps_pv", 1), ("recbc",)],
                        writes=[("ybst", 0)])
                    p.emit("pool", lambda h, pair=pair, t0=t0:
                           h.dma_start(out=yfm[64:128, pair, t0:t0 + 512],
                                       in_=ybst),
                           reads=[("ybst", 0)],
                           writes=[("yfm", pair, tc, 1)], dma="yb0")

        def pv_emit(pair, tc, si, smax):
            pslot = si % npb
            for head in range(2):
                vap, vkey = cfg["v_ap"](pair, head, si)
                p.emit("pe", lambda h, vap=vap, head=head, si=si,
                       pslot=pslot, smax=smax, hb=head * 512: h.matmul(
                           ps_pv[head][:], vap,
                           pbuf[:, pslot, hb:hb + 512],
                           start=(si == 0), stop=(si == smax - 1)),
                       reads=[vkey, cfg["vones_key"], ("pbuf", pslot)],
                       writes=[("ps_pv", head)])

        pend = None
        for pair in range(NP):
            if cfg.get("prep_pair"):
                cfg["prep_pair"](pair)
            if pair == 0:
                qproj_emit(pair, 0)
            chunks, tail = [], []
            if cfg.get("prep_stream") and pair + 1 < NP:
                chunks = cfg["prep_stream"](pair + 1)
                tail = [lambda pair=pair: cfg["prep_tail"](pair + 1),
                        lambda pair=pair: qproj_emit(pair + 1, 0)]
            tot_si = sum((tc * 512 + 511) // 128 + 1 if cfg["causal"] else TN
                         for tc in range(TCN))
            every = max(1, tot_si // (len(chunks) + 1)) if chunks else 0
            gsi = 0
            for tc in range(TCN):
                t0 = tc * 512
                par = tc % 2
                qpar = qslot(pair, tc)
                smax = (t0 + 511) // 128 + 1 if cfg["causal"] else TN
                qnext = min(smax - 1, max(4, smax - 4))
                for si in range(smax):
                    s0 = si * 128
                    sp = si % 2
                    straddle = cfg["causal"] and (s0 + 127 > t0)
                    pslot = si % npb
                    for head in range(2):
                        hb = head * 64
                        kap, kkey = cfg["k_ap"](pair, head, si)
                        p.emit("pe", lambda h, kap=kap, hb=hb, sp=sp,
                               qpar=qpar:
                               h.matmul(ps_s[sp][:, hb * 8:hb * 8 + 512], kap,
                                        qbuf[hb:hb + 64, qpar, :],
                                        start=True, stop=True),
                               reads=[kkey, ("qbuf", qpar)],
                               writes=[("ps_s", sp, hb)])
                    p.emit("act", lambda h, sp=sp, pslot=pslot: h.activation(
                        out=pbuf[:, pslot, :], in_=ps_s[sp][:],
                        func=AF.Exp, scale=cfg["scale"]),
                        reads=[("ps_s", sp, 0), ("ps_s", sp, 64)],
                        writes=[("pbuf", pslot)])
                    if straddle:
                        p.emit("pool", lambda h, s0=s0, t0=t0, pslot=pslot:
                               h.affine_select(
                                   out=pbuf[:, pslot, :], in_=pbuf[:, pslot, :],
                                   pattern=[[0, 2], [1, 512]],
                                   compare_op=ALU.is_ge,
                                   fill=0.0, base=t0 - s0,
                                   channel_multiplier=-1),
                               reads=[("pbuf", pslot)], writes=[("pbuf", pslot)])
                    if si == 2 and pend is not None:
                        pend()
                        pend = None
                    gsi += 1
                    if chunks and every and gsi % every == 0:
                        chunks.pop(0)()
                    if tail and tc == TCN - 1 and si in (4, 12):
                        tail.pop(0)()
                    if si == qnext:
                        if tc + 1 < TCN:
                            qproj_emit(pair, tc + 1)
                        elif pair + 1 < NP and not cfg.get("prep_pair") \
                                and not cfg.get("prep_stream"):
                            qproj_emit(pair + 1, 0)
                    if si >= LAG:
                        pv_emit(pair, tc, si - LAG, smax)
                if pend is not None:
                    pend()
                    pend = None
                for si in range(max(0, smax - LAG), smax):
                    pv_emit(pair, tc, si, smax)
                pend = (lambda pair=pair, tc=tc: epi_emit(pair, tc))
            for f in chunks:
                f()
            for f in tail:
                f()
        if pend is not None:
            pend()

    def proj_epi(self, yfm, wproj, wpkey, bff_row, x_in, x_out,
                 xin_name, xout_name, banks_all):
        p, t = self.p, self.t
        banks = [banks_all[i:i + 2] for i in range(0, len(banks_all), 2)]
        nb = len(banks)
        for tp in range(0, TN, 2):
            pair = []
            for tt in (tp, tp + 1):
                parts = []
                for nh in range(2):
                    bk, bkey = banks[tt % nb][nh]
                    for j in range(DC):
                        rd = [("yfm", j, tt // 4, 0), ("yfm", j, tt // 4, 1)]
                        p.emit("pe", lambda h, j=j, nh=nh, tt=tt, bk=bk: h.matmul(
                            bk, yfm[:, j, tt * 128:(tt + 1) * 128],
                            wproj[:, j, nh * 512:(nh + 1) * 512],
                            start=(j == 0),
                            stop=(j == DC - 1 and bff_row is None)),
                            reads=[wpkey] + rd, writes=[bkey])
                    if bff_row is not None:
                        p.emit("pe", lambda h, nh=nh, bk=bk: h.matmul(
                            bk, t["ones_b"][0:1, 0:128],
                            bff_row[0:1, nh * 512:(nh + 1) * 512],
                            start=False, stop=True),
                            reads=[("bff",), ("ones_b",)], writes=[bkey])
                    parts.append((bk, 512, bkey))
                pair.append((tt, parts))
            self.epilogue_tiles(pair, x_in, x_out, xin_name, xout_name)

    def phase_a0(self, x_in, x_out, w, dbg=None):
        nc, p, t = self.nc, self.p, self.t
        with ExitStack() as ph:
            P = ph.enter_context
            ph2 = ExitStack()
            P2 = ph2.enter_context
            lat = P2(nc.sbuf_tensor(self._u("lat"), [128, LC, T], BF16))
            ufm0 = P2(nc.sbuf_tensor(self._u("ufm0"), [128, 2, DC, 512], BF16))
            k_sh = P2(nc.sbuf_tensor(self._u("k_sh"), [128, T], BF16))
            v_ext = P2(nc.sbuf_tensor(self._u("v_ext"), [128, TN, 65], BF16))
            qbuf = P2(nc.sbuf_tensor(self._u("qbuf"), [128, 2, 512], BF16))
            pbuf = P2(nc.sbuf_tensor(self._u("pbuf"), [128, 4, 1024], BF16))
            recrow = P2(nc.sbuf_tensor(self._u("recrow"), [128, 512], F32))
            recrow_bf = P2(nc.sbuf_tensor(self._u("recrow_bf"), [128, 512], BF16))
            recbc = P2(nc.sbuf_tensor(self._u("recbc"), [64, 512], F32))
            ybst = P2(nc.sbuf_tensor(self._u("ybst"), [64, 2, 512], BF16))
            wcd = P2(nc.sbuf_tensor(self._u("wcd"), [128, DC, 512], BF16))
            wqu = P2(nc.sbuf_tensor(self._u("wqu"), [128, LC, D], BF16))
            wkv = P2(nc.sbuf_tensor(self._u("wkv"), [128, LC, 128], BF16))

            self.trash = lambda w: t["u_fm"][:, 0, 0:w]
            p.begin()
            with ExitStack() as sg:
                S = sg.enter_context
                ps_tr = [S(nc.psum_tensor(self._u("ps_tr0"), [128, 128], BF16)),
                         S(nc.psum_tensor(self._u("ps_tr1"), [128, 128], BF16))]
                ps_w = [S(nc.psum_tensor(self._u("ps_w0"), [128, 512], F32)),
                        S(nc.psum_tensor(self._u("ps_w1"), [128, 512], F32))]
                for nm in ("wcd", "wqu", "wkv"):
                    dst = {"wcd": wcd, "wqu": wqu, "wkv": wkv}[nm]
                    p.emit("pool", lambda h, dst=dst, nm=nm: h.dma_start(
                        out=dst[:],
                        in_=w[nm].rearrange("(c p) j -> p c j", p=128)),
                        writes=[(nm,)], dma=nm)
                p.emit("dve", lambda h: h.memset(v_ext[:, :, 64:65], 1.0),
                       writes=[("v_ones",)])
                def a0_ln(tc):
                    par = tc % 2
                    def dest(i, j, par=par):
                        return (ufm0[:, par, j, i * 128:(i + 1) * 128],
                                ("ufm0", par, j))
                    self.ln_group(x_in, tc * 4, 4, dest, ps_tr, "x_in")

                a0_ln(0)
                for tc in range(TCN):
                    par = tc % 2
                    if tc + 1 < TCN:
                        a0_ln(tc + 1)
                    for lc in range(LC):
                        pw = ps_w[lc % 2]
                        for d in range(DC):
                            p.emit("pe", lambda h, lc=lc, d=d, pw=pw, par=par:
                                   h.matmul(pw[:],
                                            wcd[:, d, lc * 128:(lc + 1) * 128],
                                            ufm0[:, par, d, :],
                                            start=(d == 0), stop=(d == DC - 1)),
                                   reads=[("wcd",), ("ufm0", par, d)],
                                   writes=[("ps_w", lc % 2)])
                        p.emit("dve", lambda h, lc=lc, tc=tc, pw=pw:
                               h.tensor_copy(
                                   lat[:, lc, tc * 512:(tc + 1) * 512], pw[:]),
                               reads=[("ps_w", lc % 2)], writes=[("lat", lc, tc)])
                    for lc in range(LC):
                        p.emit("pe", lambda h, lc=lc, tc=tc: h.matmul(
                            ps_w[0][0:64, :], wkv[:, lc, 0:64],
                            lat[:, lc, tc * 512:(tc + 1) * 512],
                            start=(lc == 0), stop=(lc == LC - 1)),
                            reads=[("wkv",), ("lat", lc, tc)],
                            writes=[("ps_w", 0)])
                    p.emit("dve", lambda h, tc=tc: h.tensor_copy(
                        k_sh[0:64, tc * 512:(tc + 1) * 512], ps_w[0][0:64, :]),
                        reads=[("ps_w", 0)], writes=[("k_lo", tc)])
                    for i in range(4):
                        tt = tc * 4 + i
                        for lc in range(LC):
                            p.emit("pe", lambda h, lc=lc, tc=tc, tt=tt: h.matmul(
                                ps_w[1][:, 0:64],
                                lat[:, lc, tt * 128:(tt + 1) * 128],
                                wkv[:, lc, 64:128],
                                start=(lc == 0), stop=(lc == LC - 1)),
                                reads=[("wkv",), ("lat", lc, tc)],
                                writes=[("ps_w", 1)])
                        p.emit("dve", lambda h, tt=tt: h.tensor_copy(
                            v_ext[:, tt, 0:64], ps_w[1][:, 0:64]),
                            reads=[("ps_w", 1)], writes=[("v_ext", tt)])
                p.emit("pool", lambda h: h.dma_start(
                    out=k_sh[64:128, :], in_=k_sh[0:64, :]),
                    reads=[("k_lo", tc) for tc in range(TCN)],
                    writes=[("k_hi",)], dma="kdup")
                if dbg:
                    dbg(1, locals())
            p.end()
            if DEBUG_A0_SEGS < 2:
                return

            p.begin()
            with ExitStack() as sg:
                S = sg.enter_context
                ps_s = [S(nc.psum_tensor(self._u("ps_sP0"), [128, 1024], F32)),
                        S(nc.psum_tensor(self._u("ps_sP1"), [128, 1024], F32))]
                ps_pv = [S(nc.psum_tensor(self._u("ps_pvA"), [65, 512], F32)),
                         S(nc.psum_tensor(self._u("ps_pvB"), [65, 512], F32))]
                ps_u = [S(nc.psum_tensor(self._u("ps_u0"), [128, 512], F32)),
                        S(nc.psum_tensor(self._u("ps_u1"), [128, 512], F32))]
                yfm = t["u_fm"]
                cfg = dict(
                    causal=True, scale=0.125,
                    k_ap=lambda pair, head, si: (
                        k_sh[head * 64:head * 64 + 64,
                             si * 128:(si + 1) * 128],
                        ("k_lo", si // 4) if head == 0 else ("k_hi",)),
                    v_ap=lambda pair, head, si: (v_ext[:, si, :], ("v_ext", si)),
                    vones_key=("v_ones",),
                    qw=lambda pair, d: (
                        wqu[:, d, pair * 128:(pair + 1) * 128], ("wqu",)),
                    q_src=lambda d, tc: (
                        lat[:, d, tc * 512:(tc + 1) * 512], ("lat", d, tc)),
                    qkc=LC, yfm=yfm, ps_s=ps_s, ps_pv=ps_pv, ps_u=ps_u,
                    pbuf=pbuf, qbuf=qbuf, recrow=recrow[64:65, :],
                    recrow_bf=recrow_bf[64:65, :],
                    recbc=recbc[:], ybst=ybst[:, 0, :], prep_pair=None)
                self.attn_pairs(cfg)
                if dbg:
                    dbg(2, locals())
            p.end()
            ph2.close()
            if DEBUG_A0_SEGS < 3:
                return

            p.begin()
            with ExitStack() as sg:
                S = sg.enter_context
                ps_x = [S(nc.psum_tensor(self._u("ps_x%d" % i), [128, 512],
                                         F32)) for i in range(8)]
                wcp = S(nc.sbuf_tensor(self._u("wcp"), [128, DC, D], BF16))
                th0 = S(nc.sbuf_tensor(self._u("trash0"), [128, 1024], BF16))
                self.trash = lambda w, th0=th0: th0[:, 0:w]
                p.emit("pool", lambda h: h.dma_start(
                    out=wcp[:],
                    in_=w["wcp"].rearrange("(c p) j -> p c j", p=128)),
                    writes=[("wcp",)], dma="wcp")
                yfm = t["u_fm"]
                banks4 = [(ps_x[i][:], ("ps_x", i)) for i in range(8)]
                self.proj_epi(yfm, wcp, ("wcp",), None, x_in, x_out,
                              "x_in", "x_out", banks4)
            p.end()

    def phase_a1(self, x_in, x_out, w):
        nc, p, t = self.nc, self.p, self.t
        with ExitStack() as ph:
            P = ph.enter_context
            yfm = P(nc.sbuf_tensor(self._u("yfm"), [128, DC, T], BF16))
            ph2 = ExitStack()
            P2 = ph2.enter_context
            kpair = P2(nc.sbuf_tensor(self._u("kpair"), [128, T], BF16))
            v_ext = P2(nc.sbuf_tensor(self._u("v_ext1"), [128, 2, TN, 130], BF16))
            qbuf = P2(nc.sbuf_tensor(self._u("qbuf1"), [128, 2, 512], BF16))
            pbuf = P2(nc.sbuf_tensor(self._u("pbuf1"), [128, 3, 1024], BF16))
            epib = P2(nc.sbuf_tensor(self._u("epib1"), [128, 1024], BF16))
            recrow = epib[64:65, 0:512]
            recrow_bf = epib[64:65, 512:1024]
            recbc = epib[0:64, 0:512]
            ybst = epib[0:64, 512:1024]
            wq_p = P2(nc.sbuf_tensor(self._u("wq_p"), [128, DC, 128], BF16))
            wkv_p = P2(nc.sbuf_tensor(self._u("wkv_p"), [128, DC, 128], BF16))
            wk_p = wv_p = wkv_p

            self.trash = lambda w: yfm[:, 0, 0:w]
            p.begin()
            with ExitStack() as sg:
                S = sg.enter_context
                ps_tr = [S(nc.psum_tensor(self._u("ps_tr0"), [128, 128], BF16)),
                         S(nc.psum_tensor(self._u("ps_tr1"), [128, 128], BF16))]
                ufm = t["u_fm"]
                for tg in range(TN // 4):
                    def dest(i, j, tg=tg):
                        tt = tg * 4 + i
                        return (ufm[:, j, tt * 128:(tt + 1) * 128],
                                ("ufm", j, tt // 4))
                    self.ln_group(x_in, tg * 4, 4, dest, ps_tr, "x_in")
                p.emit("dve", lambda h: h.memset(v_ext[:, :, :, 64:65], 1.0),
                       writes=[("v_ones",)])
                p.emit("dve", lambda h: h.memset(v_ext[:, :, :, 129:130], 1.0),
                       writes=[("v_ones",)])
            p.end()

            p.begin()
            with ExitStack() as sg:
                S = sg.enter_context
                ps_s = [S(nc.psum_tensor(self._u("ps_sP0"), [128, 1024], F32)),
                        S(nc.psum_tensor(self._u("ps_sP1"), [128, 1024], F32))]
                ps_pv = [S(nc.psum_tensor(self._u("ps_pvA"), [65, 512], F32)),
                         S(nc.psum_tensor(self._u("ps_pvB"), [65, 512], F32))]
                ps_u = [S(nc.psum_tensor(self._u("ps_u0"), [128, 512], F32)),
                        S(nc.psum_tensor(self._u("ps_u1"), [128, 512], F32))]
                ufm = t["u_fm"]

                def wslice_load(nm, dst, pair):
                    p.emit("pool", lambda h, nm=nm, dst=dst, pair=pair:
                           h.dma_start(
                               out=dst[:],
                               in_=w[nm].rearrange("(c p) j -> p c j",
                                                   p=128)[
                                   :, :, pair * 128:(pair + 1) * 128]),
                           writes=[("wkv_p",) if nm != "wq" else ("wq_p",)], dma=nm + "p")

                def kproj_tc(pair, tc):
                    for d in range(DC):
                        p.emit("pe", lambda h, d=d, tc=tc: h.matmul(
                            ps_u[0][:], wk_p[:, d, :],
                            ufm[:, d, tc * 512:(tc + 1) * 512],
                            start=(d == 0), stop=(d == DC - 1)),
                            reads=[("wkv_p",), ("ufm", d, tc)],
                            writes=[("ps_u", 0)])
                    p.emit("dve", lambda h, tc=tc: h.tensor_copy(
                        kpair[:, tc * 512:(tc + 1) * 512], ps_u[0][:]),
                        reads=[("ps_u", 0)], writes=[("kpair", tc)])

                def vproj_tt(pair, tt):
                    vs = pair % 2
                    for d in range(DC):
                        p.emit("pe", lambda h, d=d, tt=tt: h.matmul(
                            ps_u[1][:, 0:128],
                            ufm[:, d, tt * 128:(tt + 1) * 128],
                            wv_p[:, d, :],
                            start=(d == 0), stop=(d == DC - 1)),
                            reads=[("wkv_p",), ("ufm", d, tt // 4)],
                            writes=[("ps_u", 1)])
                    p.emit("dve", lambda h, tt=tt, vs=vs: h.tensor_copy(
                        v_ext[:, vs, tt, 0:64], ps_u[1][:, 0:64]),
                        reads=[("ps_u", 1)], writes=[("v_ext", vs, tt)])
                    p.emit("dve", lambda h, tt=tt, vs=vs: h.tensor_copy(
                        v_ext[:, vs, tt, 65:129], ps_u[1][:, 64:128]),
                        reads=[("ps_u", 1)], writes=[("v_ext", vs, tt)])

                def prep_pair(pair):
                    if pair == 0:
                        wslice_load("wq", wq_p, pair)
                    wslice_load("wk", wk_p, pair)
                    for tc in range(TCN):
                        kproj_tc(pair, tc)
                    if pair == 0:
                        wslice_load("wv", wv_p, pair)
                        for tt in range(TN):
                            vproj_tt(pair, tt)

                def prep_stream(pair):
                    ch = [lambda: wslice_load("wv", wv_p, pair)]
                    ch += [lambda tt=tt: vproj_tt(pair, tt)
                           for tt in range(TN)]
                    return ch

                cfg = dict(
                    causal=False, scale=1.0 / 32.0, lag=2,
                    k_ap=lambda pair, head, si: (
                        kpair[head * 64:head * 64 + 64,
                              si * 128:(si + 1) * 128],
                        ("kpair", si // 4)),
                    v_ap=lambda pair, head, si: (
                        v_ext[:, pair % 2, si, head * 65:head * 65 + 65],
                        ("v_ext", pair % 2, si)),
                    vones_key=("v_ones",),
                    qw=lambda pair, d: (wq_p[:, d, :], ("wq_p",)),
                    q_src=lambda d, tc: (
                        ufm[:, d, tc * 512:(tc + 1) * 512], ("ufm", d, tc)),
                    qkc=DC, yfm=yfm, ps_s=ps_s, ps_pv=ps_pv, ps_u=ps_u,
                    pbuf=pbuf, qbuf=qbuf, recrow=recrow, recrow_bf=recrow_bf,
                    recbc=recbc, ybst=ybst, prep_pair=prep_pair,
                    prep_stream=prep_stream,
                    prep_tail=lambda pair: wslice_load("wq", wq_p, pair))
                self.attn_pairs(cfg)
            p.end()
            ph2.close()

            self.trash = lambda w: t["u_fm"][:, 0, 0:w]
            p.begin()
            with ExitStack() as sg:
                S = sg.enter_context
                ps_x = [S(nc.psum_tensor(self._u("ps_x%d" % i), [128, 512],
                                             F32)) for i in range(8)]
                wff = S(nc.sbuf_tensor(self._u("wff"), [128, DC, D], BF16))
                bffr = S(nc.sbuf_tensor(self._u("bffr"), [1, D], BF16))
                p.emit("pool", lambda h: h.dma_start(
                    out=bffr[:], in_=w["bff"][:, :]),
                    writes=[("bff",)], dma="bff")
                p.emit("pool", lambda h: h.dma_start(
                    out=wff[:],
                    in_=w["wff"].rearrange("(c p) j -> p c j", p=128)),
                    writes=[("wff",)], dma="wffl")
                banks4 = [(ps_x[i][:], ("ps_x", i)) for i in range(8)]
                self.proj_epi(yfm, wff, ("wff",), bffr, x_in, x_out,
                              "x_in", "x_out", banks4)
            p.end()

    def phase_f0(self, x_in, x_out, w):
        nc, p, t = self.nc, self.p, self.t
        with ExitStack() as ph:
            P = ph.enter_context
            acc = P(nc.sbuf_tensor(self._u("acc"), [128, DC, D], F32))
            act_f = P(nc.sbuf_tensor(self._u("act_f"), [128, DC, 1024], BF16))
            sg_b = P(nc.sbuf_tensor(self._u("sg_b"), [128, 2, 512], BF16))
            cw_sb = P(nc.sbuf_tensor(self._u("cw_sb"), [128, 2, 8, 8], F32))
            prob = P(nc.sbuf_tensor(self._u("prob"), [128, 2, 8], F32))
            gsml = P(nc.sbuf_tensor(self._u("gsml"), [128, 2, 8], F32))
            gst = P(nc.sbuf_tensor(self._u("gst"), [128, 24], F32))
            gw_sb = P(nc.sbuf_tensor(self._u("gw_sb"), [128, DC, 8], BF16))
            wg = P(nc.sbuf_tensor(self._u("wg"), [128, DC, IM], BF16))
            wu = P(nc.sbuf_tensor(self._u("wu"), [128, DC, IM], BF16))
            wd = P(nc.sbuf_tensor(self._u("wd"), [128, DC, D], BF16))
            thf0 = P(nc.sbuf_tensor(self._u("thf0"), [128, 1024], BF16))
            self.trash = lambda w, thf0=thf0: thf0[:, 0:w]

            p.begin()
            with ExitStack() as sg:
                S = sg.enter_context
                ps_tr = [S(nc.psum_tensor(self._u("ps_tr0"), [128, 128], BF16)),
                         S(nc.psum_tensor(self._u("ps_tr1"), [128, 128], BF16))]
                ps_h = [S(nc.psum_tensor(self._u("ps_h0"), [128, 512], F32)),
                        S(nc.psum_tensor(self._u("ps_h1"), [128, 512], F32))]
                ps_d = [S(nc.psum_tensor(self._u("ps_d0"), [128, 512], F32)),
                        S(nc.psum_tensor(self._u("ps_d1"), [128, 512], F32))]
                ps_u = [S(nc.psum_tensor(self._u("ps_u0"), [128, 512], F32)),
                        S(nc.psum_tensor(self._u("ps_u1"), [128, 512], F32))]
                ufm = t["u_fm"]
                st = gst
                p.emit("pool", lambda h: h.dma_start(
                    out=gw_sb[:],
                    in_=w["gw"].rearrange("(c p) e -> p c e", p=128)),
                    writes=[("gw",)], dma="gwl")

                def wload(nm, dst, e, cols, grp):
                    src = w[nm][e]
                    K = src.shape[0]
                    if K % 128 == 0:
                        p.emit("pool", lambda h: h.dma_start(
                            out=dst[:, 0:K // 128, 0:cols],
                            in_=src.rearrange("(c p) j -> p c j", p=128)),
                            writes=[(grp,)], dma=grp)
                    else:
                        full = (K // 128) * 128
                        p.emit("pool", lambda h: h.dma_start(
                            out=dst[:, 0:full // 128, 0:cols],
                            in_=src[0:full].rearrange("(c p) j -> p c j", p=128)),
                            writes=[(grp,)], dma=grp)
                        p.emit("pool", lambda h: h.dma_start(
                            out=dst[0:K - full, full // 128, 0:cols],
                            in_=src[full:K]),
                            writes=[(grp,)], dma=grp)

                def ln_four(tt0):
                    def dest(i, j, tt0=tt0):
                        tt = tt0 + i
                        return (ufm[:, j, tt * 128:(tt + 1) * 128],
                                ("ufm", j, tt // 4))
                    self.ln_group(x_in, tt0, 4, dest, ps_tr, "x_in")

                def ln_chunk(tci):
                    ln_four(tci * 8)
                    ln_four(tci * 8 + 4)

                def gate_chunk(tci):
                    for i in range(8):
                        gate_one(tci, i)

                def gate_one(tci, i):
                    cpar = tci % 2
                    if True:
                        tt = tci * 8 + i
                        gpar = i % 2
                        for j in range(DC):
                            p.emit("pe", lambda h, j=j, tt=tt: h.matmul(
                                ps_u[0][:, 0:8],
                                ufm[:, j, tt * 128:(tt + 1) * 128],
                                gw_sb[:, j, :],
                                start=(j == 0), stop=(j == DC - 1)),
                                reads=[("gw",), ("ufm", j, tt // 4)],
                                writes=[("ps_u", 0)])
                        ko = ("cwt", gpar)
                        p.emit("act", lambda h, gpar=gpar: h.activation(
                            out=prob[:, gpar, :], in_=ps_u[0][:, 0:8],
                            func=AF.Exp, accum_out=st[:, 8 + gpar:9 + gpar]),
                            reads=[("ps_u", 0)], writes=[ko])
                        p.emit("dve", lambda h, gpar=gpar: h.reciprocal(
                            st[:, 10 + gpar:11 + gpar],
                            st[:, 8 + gpar:9 + gpar]),
                            reads=[ko], writes=[ko])
                        p.emit("dve", lambda h, gpar=gpar: h.tensor_scalar(
                            out=prob[:, gpar, :], in0=prob[:, gpar, :],
                            scalar1=st[:, 10 + gpar:11 + gpar], scalar2=None,
                            op0=ALU.mult, op1=ALU.bypass),
                            reads=[ko], writes=[ko])
                        p.emit("dve", lambda h, gpar=gpar: h.reduce_max(
                            st[:, 12 + gpar:13 + gpar], prob[:, gpar, :],
                            axis=mybir.AxisListType.X),
                            reads=[ko], writes=[ko])
                        p.emit("dve", lambda h, gpar=gpar: h.tensor_scalar(
                            out=gsml[:, gpar, :], in0=prob[:, gpar, :],
                            scalar1=st[:, 12 + gpar:13 + gpar], scalar2=None,
                            op0=ALU.is_equal, op1=ALU.bypass),
                            reads=[ko], writes=[ko])
                        p.emit("dve", lambda h, gpar=gpar: h.scalar_tensor_tensor(
                            out=gsml[:, gpar, :], in0=gsml[:, gpar, :],
                            scalar=-2.0, in1=prob[:, gpar, :],
                            op0=ALU.mult, op1=ALU.add),
                            reads=[ko], writes=[ko])
                        p.emit("dve", lambda h, gpar=gpar: h.reduce_max(
                            st[:, 14 + gpar:15 + gpar], gsml[:, gpar, :],
                            axis=mybir.AxisListType.X),
                            reads=[ko], writes=[ko])
                        p.emit("dve", lambda h, gpar=gpar: h.tensor_tensor(
                            out=st[:, 16 + gpar:17 + gpar],
                            in0=st[:, 12 + gpar:13 + gpar],
                            in1=st[:, 14 + gpar:15 + gpar], op=ALU.add),
                            reads=[ko], writes=[ko])
                        p.emit("dve", lambda h, gpar=gpar: h.tensor_scalar(
                            out=st[:, 16 + gpar:17 + gpar],
                            in0=st[:, 16 + gpar:17 + gpar],
                            scalar1=1e-8, scalar2=None,
                            op0=ALU.add, op1=ALU.bypass),
                            reads=[ko], writes=[ko])
                        p.emit("dve", lambda h, gpar=gpar: h.reciprocal(
                            st[:, 18 + gpar:19 + gpar],
                            st[:, 16 + gpar:17 + gpar]),
                            reads=[ko], writes=[ko])
                        p.emit("dve", lambda h, gpar=gpar: h.tensor_scalar(
                            out=gsml[:, gpar, :], in0=prob[:, gpar, :],
                            scalar1=st[:, 14 + gpar:15 + gpar], scalar2=None,
                            op0=ALU.is_ge, op1=ALU.bypass),
                            reads=[ko], writes=[ko])
                        p.emit("dve", lambda h, gpar=gpar: h.tensor_tensor(
                            out=gsml[:, gpar, :], in0=gsml[:, gpar, :],
                            in1=prob[:, gpar, :], op=ALU.mult),
                            reads=[ko], writes=[ko])
                        p.emit("dve", lambda h, gpar=gpar, i=i, cpar=cpar:
                               h.tensor_scalar(
                                   out=cw_sb[:, cpar, i, :],
                                   in0=gsml[:, gpar, :],
                                   scalar1=st[:, 18 + gpar:19 + gpar],
                                   scalar2=None,
                                   op0=ALU.mult, op1=ALU.bypass),
                               reads=[ko], writes=[("cw", cpar, i)])

                def epi_pair(tci, i):
                    pair = []
                    for k in (i, i + 1):
                        tt = tci * 8 + k
                        pair.append((tt, [(acc[:, k, 0:512], 512, ("acc", k, 0)),
                                          (acc[:, k, 512:1024], 512,
                                           ("acc", k, 1))]))
                    self.epilogue_tiles(pair, x_in, x_out, "x_in", "x_out")

                def epi_chunk(tci):
                    for i in range(0, 8, 2):
                        epi_pair(tci, i)

                def experts(tci, inj):
                    cpar = tci % 2
                    for e in range(9):
                        wload("gug", wg, e, IM, "wg")
                        wload("guu", wu, e, IM, "wu")
                        wload("dwn", wd, e, D, "wd")
                        for th in range(2):
                            for f in inj.get((e, th), ()):
                                f()
                            tc = tci * 2 + th
                            for fc in range(DC):
                                fw = ICW[fc]
                                for d in range(DC):
                                    p.emit("pe", lambda h, fc=fc, fw=fw, d=d,
                                           tc=tc: h.matmul(
                                               ps_h[0][0:fw, :],
                                               wg[:, d, fc * 128:fc * 128 + fw],
                                               ufm[:, d,
                                                   tc * 512:(tc + 1) * 512],
                                               start=(d == 0),
                                               stop=(d == DC - 1)),
                                           reads=[("wg",), ("ufm", d, tc)],
                                           writes=[("ps_h", 0)])
                                for d in range(DC):
                                    p.emit("pe", lambda h, fc=fc, fw=fw, d=d,
                                           tc=tc: h.matmul(
                                               ps_h[1][0:fw, :],
                                               wu[:, d, fc * 128:fc * 128 + fw],
                                               ufm[:, d,
                                                   tc * 512:(tc + 1) * 512],
                                               start=(d == 0),
                                               stop=(d == DC - 1)),
                                           reads=[("wu",), ("ufm", d, tc)],
                                           writes=[("ps_h", 1)])
                                spar = fc % 2
                                p.emit("act", lambda h, fw=fw, spar=spar:
                                       h.activation(
                                           out=sg_b[0:fw, spar, :],
                                           in_=ps_h[0][0:fw, :], func=AF.Silu),
                                       reads=[("ps_h", 0)],
                                       writes=[("sg", spar)])
                                p.emit("dve", lambda h, fw=fw, fc=fc, th=th,
                                       spar=spar: h.tensor_tensor(
                                           out=act_f[0:fw, fc,
                                                     th * 512:(th + 1) * 512],
                                           in0=ps_h[1][0:fw, :],
                                           in1=sg_b[0:fw, spar, :],
                                           op=ALU.mult),
                                       reads=[("ps_h", 1), ("sg", spar)],
                                       writes=[("act_f", fc)])
                        for i in range(8):
                            for f in inj.get((e, "dn", i), ()):
                                f()
                            for nh in range(2):
                                dpar = (i * 2 + nh) % 2
                                for ic in range(DC):
                                    iw = ICW[ic]
                                    p.emit("pe", lambda h, ic=ic, iw=iw, i=i,
                                           nh=nh, dpar=dpar: h.matmul(
                                               ps_d[dpar][:],
                                               act_f[0:iw, ic,
                                                     i * 128:(i + 1) * 128],
                                               wd[0:iw, ic,
                                                  nh * 512:(nh + 1) * 512],
                                               start=(ic == 0),
                                               stop=(ic == DC - 1)),
                                           reads=[("wd",), ("act_f", ic)],
                                           writes=[("ps_d", dpar)])
                                if e == 0:
                                    p.emit("dve", lambda h, i=i, nh=nh,
                                           dpar=dpar: h.tensor_copy(
                                               acc[:, i,
                                                   nh * 512:(nh + 1) * 512],
                                               ps_d[dpar][:]),
                                           reads=[("ps_d", dpar)],
                                           writes=[("acc", i, nh)])
                                else:
                                    p.emit("dve", lambda h, i=i, nh=nh, e=e,
                                           dpar=dpar, cpar=cpar:
                                           h.scalar_tensor_tensor(
                                               out=acc[:, i,
                                                       nh * 512:(nh + 1) * 512],
                                               in0=ps_d[dpar][:],
                                               scalar=cw_sb[:, cpar, i, e - 1:e],
                                               in1=acc[:, i,
                                                       nh * 512:(nh + 1) * 512],
                                               op0=ALU.mult, op1=ALU.add),
                                           reads=[("ps_d", dpar),
                                                  ("cw", cpar, i),
                                                  ("acc", i, nh)],
                                           writes=[("acc", i, nh)])

                ln_chunk(0)
                gate_chunk(0)
                slots = [(e, th) for e in range(9) for th in range(2)]
                for tci in range(4):
                    inj = {}
                    def add(slot, f):
                        inj.setdefault(slot, []).append(f)
                    pass
                    if tci < 3:
                        add(slots[8], lambda tci=tci: ln_four((tci + 1) * 8))
                        add(slots[11], lambda tci=tci: ln_four((tci + 1) * 8 + 4))
                        for i in range(8):
                            add(slots[10 + i] if 10 + i < 18 else slots[17],
                                lambda tci=tci, i=i: gate_one(tci + 1, i))
                    experts(tci, inj)
                    epi_chunk(tci)
            p.end()

    def phase_f1(self, x_in, x_out, w):
        nc, p, t = self.nc, self.p, self.t
        with ExitStack() as ph:
            P = ph.enter_context
            xr = P(nc.sbuf_tensor(self._u("xr"), [128, DC, 512], BF16))
            xk = P(nc.sbuf_tensor(self._u("xk"), [128, DC, 512], BF16))
            ka = P(nc.sbuf_tensor(self._u("ka"), [128, DC, 512], BF16))
            r_sb = P(nc.sbuf_tensor(self._u("r_sb"), [128, 4, D], BF16))
            o_sb = P(nc.sbuf_tensor(self._u("o_sb"), [128, 2, D], F32))
            rk_t = P(nc.sbuf_tensor(self._u("rk_t"), [128, 2, 512], F32))
            mixt = P(nc.sbuf_tensor(self._u("mixt"), [128, 2, 512], BF16))
            mu_r = P(nc.sbuf_tensor(self._u("mu_r"), [128, DC], F32))
            mu_k = P(nc.sbuf_tensor(self._u("mu_k"), [128, DC], F32))
            om_r = P(nc.sbuf_tensor(self._u("om_r"), [128, DC], F32))
            om_k = P(nc.sbuf_tensor(self._u("om_k"), [128, DC], F32))
            wr = P(nc.sbuf_tensor(self._u("wr"), [128, DC, D], BF16))
            wkk = P(nc.sbuf_tensor(self._u("wkk"), [128, DC, D], BF16))
            wvv = P(nc.sbuf_tensor(self._u("wvv"), [128, DC, D], BF16))
            thf1 = P(nc.sbuf_tensor(self._u("thf1"), [128, 1024], BF16))
            self.trash = lambda w, thf1=thf1: thf1[:, 0:w]

            p.begin()
            with ExitStack() as sg:
                S = sg.enter_context
                ps_tr = [S(nc.psum_tensor(self._u("ps_tr0"), [128, 128], BF16)),
                         S(nc.psum_tensor(self._u("ps_tr1"), [128, 128], BF16))]
                ps_g = [S(nc.psum_tensor(self._u("ps_g%d" % i), [128, 512], F32))
                        for i in range(4)]
                ufm = t["u_fm"]
                for nm, dst in (("wr", wr), ("wkk", wkk), ("wvv", wvv)):
                    p.emit("pool", lambda h, nm=nm, dst=dst: h.dma_start(
                        out=dst[:],
                        in_=w[nm].rearrange("(c p) j -> p c j", p=128)),
                        writes=[(nm,)], dma=nm)
                for nm, dst in (("mur", mu_r), ("muk", mu_k)):
                    p.emit("sp", lambda h, nm=nm, dst=dst: h.dma_start(
                        out=dst[:], in_=w[nm][:, :]),
                        writes=[(nm,)], dma="mu")
                p.emit("dve", lambda h: h.tensor_scalar(
                    out=om_r[:], in0=mu_r[:], scalar1=-1.0, scalar2=1.0,
                    op0=ALU.mult, op1=ALU.add),
                    reads=[("mur",)], writes=[("omr",)])
                p.emit("dve", lambda h: h.tensor_scalar(
                    out=om_k[:], in0=mu_k[:], scalar1=-1.0, scalar2=1.0,
                    op0=ALU.mult, op1=ALU.add),
                    reads=[("muk",)], writes=[("omk",)])
                for tg in range(TN // 4):
                    def dest(i, j, tg=tg):
                        tt = tg * 4 + i
                        return (ufm[:, j, tt * 128:(tt + 1) * 128],
                                ("ufm", j, tt // 4))
                    self.ln_group(x_in, tg * 4, 4, dest, ps_tr, "x_in")
                def mixes(tc):
                    t0 = tc * 512
                    for cc in range(DC):
                        for which, mu, om, dst, dk in (
                                (0, mu_r, om_r, xr, "xr"), (1, mu_k, om_k, xk, "xk")):
                            mp = which
                            p.emit("act", lambda h, cc=cc, mu=mu, mp=mp, t0=t0:
                                   h.activation(
                                       out=mixt[:, mp, :],
                                       in_=ufm[:, cc, t0:t0 + 512],
                                       func=AF.Copy, scale=mu[:, cc:cc + 1]),
                                   reads=[("ufm", cc, tc)],
                                   writes=[("mixt", mp)])
                            if tc == 0:
                                p.emit("dve", lambda h, cc=cc, om=om, dst=dst,
                                       mp=mp: h.scalar_tensor_tensor(
                                           out=dst[:, cc, 1:512],
                                           in0=ufm[:, cc, 0:511],
                                           scalar=om[:, cc:cc + 1],
                                           in1=mixt[:, mp, 1:512],
                                           op0=ALU.mult, op1=ALU.add),
                                       reads=[("ufm", cc, tc), ("mixt", mp)],
                                       writes=[(dk, cc)])
                                p.emit("dve", lambda h, cc=cc, dst=dst, mp=mp:
                                       h.tensor_copy(dst[:, cc, 0:1],
                                                     mixt[:, mp, 0:1]),
                                       reads=[("mixt", mp)], writes=[(dk, cc)])
                            else:
                                p.emit("dve", lambda h, cc=cc, om=om, dst=dst,
                                       mp=mp, t0=t0: h.scalar_tensor_tensor(
                                           out=dst[:, cc, :],
                                           in0=ufm[:, cc, t0 - 1:t0 + 511],
                                           scalar=om[:, cc:cc + 1],
                                           in1=mixt[:, mp, :],
                                           op0=ALU.mult, op1=ALU.add),
                                       reads=[("ufm", cc, tc),
                                              ("ufm", cc, tc - 1), ("mixt", mp)],
                                       writes=[(dk, cc)])
                mixes(0)
                for tc in range(TCN):
                    t0 = tc * 512
                    for i in range(4):
                        for nh in range(2):
                            gp = (i * 2 + nh) % 2
                            for cc in range(DC):
                                p.emit("pe", lambda h, cc=cc, i=i, nh=nh, gp=gp:
                                       h.matmul(
                                           ps_g[gp][:],
                                           xr[:, cc, i * 128:(i + 1) * 128],
                                           wr[:, cc, nh * 512:(nh + 1) * 512],
                                           start=(cc == 0), stop=(cc == DC - 1)),
                                       reads=[("wr",), ("xr", cc)],
                                       writes=[("ps_g", gp)])
                            p.emit("act", lambda h, i=i, nh=nh, gp=gp:
                                   h.activation(
                                       out=r_sb[:, i, nh * 512:(nh + 1) * 512],
                                       in_=ps_g[gp][:], func=AF.Sigmoid),
                                   reads=[("ps_g", gp)], writes=[("r", i)])
                    for mc in range(DC):
                        gp = 2 + mc % 2
                        for cc in range(DC):
                            p.emit("pe", lambda h, cc=cc, mc=mc, gp=gp: h.matmul(
                                ps_g[gp][:],
                                wkk[:, cc, mc * 128:(mc + 1) * 128],
                                xk[:, cc, :],
                                start=(cc == 0), stop=(cc == DC - 1)),
                                reads=[("wkk",), ("xk", cc)],
                                writes=[("ps_g", gp)])
                        rp = mc % 2
                        p.emit("act", lambda h, gp=gp, rp=rp: h.activation(
                            out=rk_t[:, rp, :], in_=ps_g[gp][:], func=AF.Relu),
                            reads=[("ps_g", gp)], writes=[("rk_t", rp)])
                        p.emit("act", lambda h, mc=mc, rp=rp: h.activation(
                            out=ka[:, mc, :], in_=rk_t[:, rp, :],
                            func=AF.Square),
                            reads=[("rk_t", rp)], writes=[("ka", mc)])
                    if tc + 1 < TCN:
                        mixes(tc + 1)
                    for i in range(4):
                        tt = tc * 4 + i
                        osl = i % 2
                        for nh in range(2):
                            gp = nh
                            for cc in range(DC):
                                p.emit("pe", lambda h, cc=cc, i=i, nh=nh, gp=gp:
                                       h.matmul(
                                           ps_g[gp][:],
                                           ka[:, cc, i * 128:(i + 1) * 128],
                                           wvv[:, cc, nh * 512:(nh + 1) * 512],
                                           start=(cc == 0), stop=(cc == DC - 1)),
                                       reads=[("wvv",), ("ka", cc)],
                                       writes=[("ps_g", gp)])
                            p.emit("dve", lambda h, i=i, nh=nh, gp=gp, osl=osl:
                                   h.tensor_tensor(
                                       out=o_sb[:, osl, nh * 512:(nh + 1) * 512],
                                       in0=ps_g[gp][:],
                                       in1=r_sb[:, i, nh * 512:(nh + 1) * 512],
                                       op=ALU.mult),
                                   reads=[("ps_g", gp), ("r", i)],
                                   writes=[("o", osl)])
                        if i % 2 == 1:
                            self.epilogue_tiles(
                                [(tt - 1, [(o_sb[:, 0, :], D, ("o", 0))]),
                                 (tt, [(o_sb[:, 1, :], D, ("o", 1))])],
                                x_in, x_out, "x_in", "x_out")
            p.end()


PHASE_WEIGHTS = {
    "a0": ["wcd", "wqu", "wkv", "wcp"],
    "f0": ["gw", "gug", "guu", "dwn"],
    "a1": ["wq", "wk", "wv", "wff", "bff"],
    "f1": ["wr", "wkk", "wvv", "mur", "muk"],
}
WSHAPES = {
    "wcd": [D, 512], "wqu": [512, D], "wkv": [512, 128], "wcp": [D, D],
    "gw": [D, 8], "gug": [9, D, IM], "guu": [9, D, IM], "dwn": [9, IM, D],
    "wq": [D, D], "wk": [D, D], "wv": [D, D], "wff": [D, D], "bff": [1, D],
    "wr": [D, D], "wkk": [D, D], "wvv": [D, D], "mur": [128, 8], "muk": [128, 8],
}


def build_program(phases, first, last):
    nc = bass.Bass("TRN2", target_bir_lowering=False,
                   detect_race_conditions=False)
    names = {"a0": ("x0", "x1"), "f0": ("x1", "x2"),
             "a1": ("x2", "x3"), "f1": ("x3", "y")}
    dram = {}
    xio = {}
    for i, phn in enumerate(phases):
        nin, nout = names[phn]
        if nin not in xio:
            xio[nin] = nc.dram_tensor(nin, [T, D], F32,
                                      kind="ExternalInput").ap()
        if nout not in xio:
            kind = "ExternalOutput" if (phn == phases[-1]) else "Internal"
            xio[nout] = nc.dram_tensor(nout, [T, D], F32, kind=kind).ap()
        for wn in PHASE_WEIGHTS[phn]:
            if wn not in dram:
                dram[wn] = nc.dram_tensor(wn, WSHAPES[wn], F32,
                                          kind="ExternalInput").ap()
    with ExitStack() as es:
        p = Prog(nc, es)
        b = Build(nc, p, es)
        b.alloc_common()
        p.begin()
        b.setup_common()
        p.end()
        for phn in phases:
            nin, nout = names[phn]
            w = {k: dram[k] for k in PHASE_WEIGHTS[phn]}
            getattr(b, "phase_" + phn)(xio[nin], xio[nout], w)
        p.drain()
    return nc, [names[phn][1] for phn in phases][-1]


_PROG_CACHE = {}


def _host_weights(inputs):
    C = np.ascontiguousarray
    w = {}
    w["wcd"] = C(inputs["L0_attn_Wc_down"])
    w["wqu"] = C(inputs["L0_attn_Wq_up"])
    w["wkv"] = C(inputs["L0_attn_Wkv_up"])
    w["wcp"] = C(inputs["L0_attn_Wc_proj"])
    w["gw"] = C(inputs["L0_ffn_gate_weight"].T)
    sgu = inputs["L0_ffn_shared_gate_up"]
    egu = inputs["L0_ffn_experts_gate_up"]
    w["gug"] = C(np.stack([sgu[0:IM].T] + [egu[e][0:IM].T for e in range(8)]))
    w["guu"] = C(np.stack([sgu[IM:2 * IM].T] + [egu[e][IM:2 * IM].T
                                                for e in range(8)]))
    sdn = inputs["L0_ffn_shared_down"]
    edn = inputs["L0_ffn_experts_down"]
    w["dwn"] = C(np.stack([sdn.T] + [edn[e].T for e in range(8)]))
    w["wq"] = C(inputs["L1_attn_Wq"].T)
    w["wk"] = C(inputs["L1_attn_Wk"].T)
    w["wv"] = C(inputs["L1_attn_Wv"].T)
    w["wff"] = C(inputs["L1_attn_Wff"].T)
    w["bff"] = C(inputs["L1_attn_bff"].reshape(1, D))
    w["wr"] = C(inputs["L1_ffn_W_r"].T)
    w["wkk"] = C(inputs["L1_ffn_W_k"].T)
    w["wvv"] = C(inputs["L1_ffn_W_v"].T)
    w["mur"] = C(inputs["L1_ffn_mu_r"].reshape(8, 128).T)
    w["muk"] = C(inputs["L1_ffn_mu_k"].reshape(8, 128).T)
    return w


def kernel(**inputs):
    x = np.ascontiguousarray(inputs["L0_attn_x"].astype(np.float32))
    w = _host_weights(inputs)
    ncores = 8
    xcur = {"x0": [x[c] for c in range(ncores)]}
    out = None
    for li, phases in enumerate(LAUNCH_PLAN):
        key = tuple(phases)
        if key not in _PROG_CACHE:
            _PROG_CACHE[key] = build_program(phases, li == 0,
                                             li == len(LAUNCH_PLAN) - 1)
        nc, outname = _PROG_CACHE[key]
        names = {"a0": ("x0", "x1"), "f0": ("x1", "x2"),
                 "a1": ("x2", "x3"), "f1": ("x3", "y")}
        nin = names[phases[0]][0]
        in_maps = []
        for c in range(ncores):
            m = {nin: xcur[nin][c]}
            for phn in phases:
                for wn in PHASE_WEIGHTS[phn]:
                    m[wn] = w[wn]
            in_maps.append(m)
        res = run_bass_kernel_spmd(nc, in_maps, core_ids=list(range(ncores)))
        outs = [res.results[c][outname] for c in range(ncores)]
        xcur[outname] = outs
        out = outs
    return np.stack(out).astype(np.float32)


# version 14 (final)
# speedup vs baseline: 1.1994x; 1.1994x over previous
import numpy as np
import ml_dtypes
from contextlib import ExitStack
import concourse.bass as bass
import concourse.mybir as mybir
from concourse.bass_utils import run_bass_kernel_spmd

F32 = mybir.dt.float32
BF16 = mybir.dt.bfloat16
AF = mybir.ActivationFunctionType
ALU = mybir.AluOpType

T = 4096
D = 1024
TN = 32
TCN = 8
DC = 8
LC = 4
NP = 8
IM = 938
ICW = [128] * 7 + [42]
EPS = 1e-5

ENGS = ("pe", "act", "dve", "pool", "sp")
SERIAL = False
SYNC_SAME_ENGINE = False
SYNC_ENGINES = {"dve", "act", "pool"}

LAUNCH_PLAN = [["a0", "f0", "a1", "f1"]]
DEBUG_A0_SEGS = 5
DEBUG_EPI = 9


class Prog:

    def __init__(self, nc, es):
        self.nc = nc
        self.es = es
        self.sem_h = {}
        self.count = {}
        self.observed = {e: {} for e in ENGS}
        self.cur = None
        self.n_waits = 0
        self.n_instr = 0

    def sem(self, name):
        if name not in self.sem_h:
            self.sem_h[name] = self.es.enter_context(self.nc.semaphore(name))
            self.count[name] = 0
        return name

    def begin(self):
        assert self.cur is None
        self.cur = []
        self.lastw = {}
        self.readers = {}
        self.group_prev = {}
        self.snapshot = dict(self.count)

    def emit(self, eng, fn, reads=(), writes=(), dma=None):
        ps = [k for k in reads if isinstance(k, tuple) and k
              and isinstance(k[0], str) and k[0].startswith("ps")]
        if ps:
            reads = [k for k in reads if k not in ps]
            writes = list(writes) + ps
        i = len(self.cur)
        deps = set()
        for k in reads:
            w = self.lastw.get(k)
            if w is not None:
                deps.add(w)
        for k in writes:
            w = self.lastw.get(k)
            if w is not None:
                deps.add(w)
            deps.update(self.readers.get(k, ()))
        if dma is not None:
            pidx = self.group_prev.get(dma)
            if pidx is not None:
                deps.add(pidx)
            self.group_prev[dma] = i
        if SERIAL and i > 0:
            deps.add(i - 1)
        deps.discard(i)
        self.cur.append(dict(eng=eng, fn=fn, dma=dma, deps=deps,
                             mark=False, sem=None, val=None, waits=()))
        for k in reads:
            self.readers.setdefault(k, []).append(i)
        for k in writes:
            self.lastw[k] = i
            self.readers[k] = []
        return i

    @staticmethod
    def _nw(p, c):
        if SERIAL or SYNC_SAME_ENGINE:
            return True
        if p["eng"] == c["eng"] and p["eng"] in SYNC_ENGINES:
            return True
        return (p["dma"] is not None or p["eng"] != c["eng"]
                or c["dma"] is not None)

    def end(self):
        cur = self.cur
        self.n_instr += len(cur)
        for c in cur:
            for d in c["deps"]:
                if self._nw(cur[d], c):
                    cur[d]["mark"] = True
        lasts = {}
        for rec in cur:
            if rec["dma"] is None and rec["fn"] is not None:
                lasts[rec["eng"]] = rec
        for rec in lasts.values():
            rec["mark"] = True
        for rec in cur:
            if rec["dma"] is not None:
                s = self.sem("d_" + rec["dma"])
                self.count[s] += 16
                rec["sem"], rec["val"] = s, self.count[s]
            elif rec["mark"]:
                s = self.sem("e_" + rec["eng"])
                self.count[s] += 1
                rec["sem"], rec["val"] = s, self.count[s]
        streams = {e: [] for e in ENGS}
        for rec in cur:
            streams[rec["eng"]].append(rec)
        for e, lst in streams.items():
            obs = self.observed[e]
            first = True
            for rec in lst:
                waits = {}
                if first:
                    for s, v in self.snapshot.items():
                        if v > obs.get(s, 0):
                            waits[s] = v
                    first = False
                for d in rec["deps"]:
                    p = cur[d]
                    if not self._nw(p, rec):
                        continue
                    assert p["sem"] is not None
                    if p["val"] > obs.get(p["sem"], 0) and \
                       p["val"] > waits.get(p["sem"], 0):
                        waits[p["sem"]] = p["val"]
                rec["waits"] = sorted(waits.items())
                for s, v in waits.items():
                    obs[s] = max(obs.get(s, 0), v)
                self.n_waits += len(rec["waits"])
        sem_h = self.sem_h
        with self.nc.Block() as block:
            def mk(e):
                def run(h):
                    for rec in streams[e]:
                        for s, v in rec["waits"]:
                            h.wait_ge(sem_h[s], v)
                        if rec["fn"] is None:
                            continue
                        b = rec["fn"](h)
                        if rec["sem"] is not None:
                            b.then_inc(sem_h[rec["sem"]],
                                       16 if rec["dma"] is not None else 1)
                return run
            if streams["sp"]:
                block.sync(mk("sp"))
            if streams["pool"]:
                block.gpsimd(mk("pool"))
            if streams["dve"]:
                block.vector(mk("dve"))
            if streams["act"]:
                block.scalar(mk("act"))
            if streams["pe"]:
                block.tensor(mk("pe"))
        self.cur = None

    def drain(self):
        self.begin()
        self.emit("sp", None)
        self.end()


class Build:
    def __init__(self, nc, p, es):
        self.nc = nc
        self.p = p
        self.es = es
        self.t = {}

    def _u(self, name):
        self._nid = getattr(self, "_nid", 0) + 1
        return "%s_%d" % (name, self._nid)

    def sb(self, name, shape, dtype):
        h = self.es.enter_context(self.nc.sbuf_tensor(name, list(shape), dtype))
        self.t[name] = h
        return h

    def alloc_common(self):
        sb = self.sb
        sb("u_fm", [128, DC, T], BF16)
        sb("xstage", [128, 4, D], F32)
        sb("xold", [128, 2, D], F32)
        sb("xnew", [128, 2, D], F32)
        sb("tnorm", [128, D], F32)
        sb("u_tm", [128, 2, D], BF16)
        self.xold_buf = self.t["xold"]
        self.trash = None
        sb("stats", [128, 64], F32)
        sb("estats", [128, 32], F32)
        sb("ident", [128, 128], BF16)
        sb("ones_b", [128, 128], BF16)

    def setup_common(self):
        p, t = self.p, self.t
        p.emit("dve", lambda h: h.memset(t["ones_b"][:], 1.0), writes=[("ones_b",)])
        p.emit("pool", lambda h: h.affine_select(
            out=t["ident"][:], in_=t["ones_b"][:], pattern=[[-1, 128]],
            compare_op=ALU.is_equal, fill=0.0, base=0, channel_multiplier=1),
            reads=[("ones_b",)], writes=[("ident",)])

    def ln_group(self, x_ap, tt0, n, dest_fn, ps_tr, xin_name):
        p, t = self.p, self.t
        xst, st, utm = t["xstage"], t["stats"], t["u_tm"]
        gp = (tt0 // 4) % 2
        so = gp * 32
        kb = ("lnst", gp)
        for i in range(n):
            t0 = (tt0 + i) * 128
            p.emit("sp", lambda h, i=i, t0=t0: h.dma_start(
                out=xst[:, i, :], in_=x_ap[t0:t0 + 128, :]),
                reads=[(xin_name, tt0 + i)], writes=[("xstage", i)],
                dma="xs%d" % (i % 2))
            p.emit("dve", lambda h, i=i: h.tensor_scalar(
                out=xst[:, i, :], in0=xst[:, i, :], scalar1=0.0, scalar2=None,
                op0=ALU.add, op1=ALU.add, accum_out=st[:, so + i:so + i + 1]),
                reads=[("xstage", i)], writes=[kb + ("s", i), ("xstage", i)])
            p.emit("act", lambda h, i=i: h.activation(
                out=self.trash(D), in_=xst[:, i, :], func=AF.Square,
                accum_out=st[:, so + 4 + i:so + 5 + i]),
                reads=[("xstage", i)], writes=[kb + ("q", i)])
        skeys = [kb + ("s", i) for i in range(n)] + \
                [kb + ("q", i) for i in range(n)]
        p.emit("dve", lambda h: h.tensor_scalar(
            out=st[:, so + 8:so + 8 + n], in0=st[:, so:so + n], scalar1=1.0 / D,
            scalar2=None, op0=ALU.mult, op1=ALU.bypass),
            reads=skeys, writes=[kb + ("mean",)])
        p.emit("dve", lambda h: h.tensor_scalar(
            out=st[:, so + 12:so + 12 + n], in0=st[:, so + 4:so + 4 + n],
            scalar1=1.0 / D, scalar2=EPS, op0=ALU.mult, op1=ALU.add),
            reads=skeys, writes=[kb + ("v1",)])
        p.emit("dve", lambda h: h.tensor_tensor(
            out=st[:, so + 16:so + 16 + n], in0=st[:, so + 8:so + 8 + n],
            in1=st[:, so + 8:so + 8 + n], op=ALU.mult),
            reads=[kb + ("mean",)], writes=[kb + ("m2",)])
        p.emit("dve", lambda h: h.tensor_tensor(
            out=st[:, so + 20:so + 20 + n], in0=st[:, so + 12:so + 12 + n],
            in1=st[:, so + 16:so + 16 + n], op=ALU.subtract),
            reads=[kb + ("v1",), kb + ("m2",)], writes=[kb + ("var",)])
        p.emit("act", lambda h: h.activation(
            out=st[:, so + 24:so + 24 + n], in_=st[:, so + 20:so + 20 + n],
            func=AF.Sqrt),
            reads=[kb + ("var",)], writes=[kb + ("std",)])
        p.emit("dve", lambda h: h.reciprocal(
            st[:, so + 28:so + 28 + n], st[:, so + 24:so + 24 + n]),
            reads=[kb + ("std",)], writes=[kb + ("rstd",)])
        for i in range(n):
            us = i % 2
            p.emit("dve", lambda h, i=i, us=us: h.tensor_scalar(
                out=utm[:, us, :], in0=xst[:, i, :],
                scalar1=st[:, so + 8 + i:so + 9 + i],
                scalar2=st[:, so + 28 + i:so + 29 + i],
                op0=ALU.subtract, op1=ALU.mult),
                reads=[("xstage", i), kb + ("mean",), kb + ("rstd",)],
                writes=[("u_tm", us)])
            for j in range(DC):
                pj = ps_tr[j % 2]
                p.emit("pe", lambda h, us=us, j=j, pj=pj: h.transpose(
                    pj[:], utm[:, us, j * 128:(j + 1) * 128], t["ident"][:]),
                    reads=[("u_tm", us), ("ident",)], writes=[("ps_tr", j % 2)])
                dap, dkey = dest_fn(i, j)
                if j % 2 == 0:
                    p.emit("dve", lambda h, dap=dap, pj=pj: h.tensor_copy(
                        dap, pj[:]),
                        reads=[("ps_tr", j % 2)], writes=[dkey])
                else:
                    p.emit("act", lambda h, dap=dap, pj=pj: h.activation(
                        out=dap, in_=pj[:], func=AF.Copy),
                        reads=[("ps_tr", j % 2)], writes=[dkey])

    def _stat_tail(self, st, so, kb):
        p = self.p
        p.emit("dve", lambda h: h.tensor_scalar(
            out=st[:, so + 2:so + 3], in0=st[:, so:so + 1], scalar1=1.0 / D,
            scalar2=None, op0=ALU.mult, op1=ALU.bypass),
            reads=[kb + (0,)], writes=[kb + (2,)])
        p.emit("dve", lambda h: h.tensor_scalar(
            out=st[:, so + 3:so + 4], in0=st[:, so + 1:so + 2], scalar1=1.0 / D,
            scalar2=EPS, op0=ALU.mult, op1=ALU.add),
            reads=[kb + (1,)], writes=[kb + (3,)])
        p.emit("dve", lambda h: h.tensor_tensor(
            out=st[:, so + 4:so + 5], in0=st[:, so + 2:so + 3],
            in1=st[:, so + 2:so + 3], op=ALU.mult),
            reads=[kb + (2,)], writes=[kb + (4,)])
        p.emit("dve", lambda h: h.tensor_tensor(
            out=st[:, so + 5:so + 6], in0=st[:, so + 3:so + 4],
            in1=st[:, so + 4:so + 5], op=ALU.subtract),
            reads=[kb + (3,), kb + (4,)], writes=[kb + (5,)])
        p.emit("act", lambda h: h.activation(
            out=st[:, so + 6:so + 7], in_=st[:, so + 5:so + 6], func=AF.Sqrt),
            reads=[kb + (5,)], writes=[kb + (6,)])
        p.emit("dve", lambda h: h.reciprocal(
            st[:, so + 7:so + 8], st[:, so + 6:so + 7]),
            reads=[kb + (6,)], writes=[kb + (7,)])

    def epi_loads(self, tiles, x_in_ap, xin_name):
        p = self.p
        xo = self.xold_buf
        ns = xo.shape[1]
        for tt in tiles:
            sl = tt % ns
            t0 = tt * 128
            p.emit("sp", lambda h, sl=sl, t0=t0: h.dma_start(
                out=xo[:, sl, :], in_=x_in_ap[t0:t0 + 128, :]),
                reads=[(xin_name, tt)], writes=[("xold", sl)],
                dma="xo%d" % sl)

    def epilogue_tiles(self, tiles, x_in_ap, x_out_ap, xin_name, xout_name,
                       skip_loads=False):
        p, t = self.p, self.t
        st = t["estats"]
        n = len(tiles)
        bp = (tiles[0][0] // n) % 2 if n else 0
        so = bp * 16
        kb = ("est", bp)
        xo, tn, xn = self.xold_buf, t["tnorm"], t["xnew"]
        ns = xo.shape[1]
        if not skip_loads:
            self.epi_loads([tt for tt, _ in tiles], x_in_ap, xin_name)
        pkeys = []
        for k, (tt, o_parts) in enumerate(tiles):
            t0 = tt * 128
            for i, (ap, w, key) in enumerate(o_parts):
                if len(o_parts) == 1:
                    sdst = st[:, so + k:so + k + 1]
                    qdst = st[:, so + 2 + k:so + 3 + k]
                else:
                    sdst = st[:, so + 8 + 2 * i + k:so + 9 + 2 * i + k]
                    qdst = st[:, so + 12 + 2 * i + k:so + 13 + 2 * i + k]
                p.emit("dve", lambda h, ap=ap, sdst=sdst: h.tensor_scalar(
                    out=ap, in0=ap, scalar1=0.0, scalar2=None,
                    op0=ALU.add, op1=ALU.add, accum_out=sdst),
                    reads=[key], writes=[kb + ("ps", k, i), key])
                p.emit("act", lambda h, ap=ap, w=w, qdst=qdst: h.activation(
                    out=self.trash(w), in_=ap, func=AF.Square,
                    accum_out=qdst),
                    reads=[key], writes=[kb + ("pq", k, i)])
                pkeys += [kb + ("ps", k, i), kb + ("pq", k, i)]
        if len(tiles[0][1]) == 2:
            p.emit("dve", lambda h: h.tensor_tensor(
                out=st[:, so:so + n], in0=st[:, so + 8:so + 8 + n],
                in1=st[:, so + 10:so + 10 + n], op=ALU.add),
                reads=pkeys, writes=[kb + ("s",)])
            p.emit("dve", lambda h: h.tensor_tensor(
                out=st[:, so + 2:so + 2 + n], in0=st[:, so + 12:so + 12 + n],
                in1=st[:, so + 14:so + 14 + n], op=ALU.add),
                reads=pkeys, writes=[kb + ("q",)])
        skeys = pkeys + [kb + ("s",), kb + ("q",)]
        p.emit("dve", lambda h: h.tensor_scalar(
            out=st[:, so + 4:so + 4 + n], in0=st[:, so:so + n],
            scalar1=1.0 / D, scalar2=None, op0=ALU.mult, op1=ALU.bypass),
            reads=skeys, writes=[kb + ("mean",)])
        p.emit("dve", lambda h: h.tensor_scalar(
            out=st[:, so + 6:so + 6 + n], in0=st[:, so + 2:so + 2 + n],
            scalar1=1.0 / D, scalar2=EPS, op0=ALU.mult, op1=ALU.add),
            reads=skeys, writes=[kb + ("v1",)])
        p.emit("dve", lambda h: h.tensor_tensor(
            out=st[:, so + 8:so + 8 + n], in0=st[:, so + 4:so + 4 + n],
            in1=st[:, so + 4:so + 4 + n], op=ALU.mult),
            reads=[kb + ("mean",)], writes=[kb + ("m2",)])
        p.emit("dve", lambda h: h.tensor_tensor(
            out=st[:, so + 10:so + 10 + n], in0=st[:, so + 6:so + 6 + n],
            in1=st[:, so + 8:so + 8 + n], op=ALU.subtract),
            reads=[kb + ("v1",), kb + ("m2",)], writes=[kb + ("var",)])
        p.emit("act", lambda h: h.activation(
            out=st[:, so + 12:so + 12 + n], in_=st[:, so + 10:so + 10 + n],
            func=AF.Sqrt),
            reads=[kb + ("var",)], writes=[kb + ("std",)])
        p.emit("dve", lambda h: h.reciprocal(
            st[:, so + 14:so + 14 + n], st[:, so + 12:so + 12 + n]),
            reads=[kb + ("std",)], writes=[kb + ("rstd",)])
        for k, (tt, o_parts) in enumerate(tiles):
            t0 = tt * 128
            off = 0
            tkeys = []
            for ap, w, key in o_parts:
                p.emit("dve", lambda h, ap=ap, off=off, w=w, k=k: h.tensor_scalar(
                    out=tn[:, off:off + w], in0=ap,
                    scalar1=st[:, so + 4 + k:so + 5 + k],
                    scalar2=st[:, so + 14 + k:so + 15 + k],
                    op0=ALU.subtract, op1=ALU.mult),
                    reads=[key, kb + ("mean",), kb + ("rstd",)],
                    writes=[("tnorm", off)])
                tkeys.append(("tnorm", off))
                off += w
            p.emit("dve", lambda h, k=k, sl=tt % ns: h.tensor_tensor(
                out=xn[:, k, :], in0=xo[:, sl, :], in1=tn[:, 0:D], op=ALU.add),
                reads=[("xold", tt % ns)] + tkeys, writes=[("xnew", k)])
            p.emit("sp", lambda h, k=k, t0=t0: h.dma_start(
                out=x_out_ap[t0:t0 + 128, :], in_=xn[:, k, :]),
                reads=[("xnew", k)], writes=[(xout_name, tt)], dma="xn%d" % k)

    def epilogue_tile(self, o_parts, x_in_ap, x_out_ap, tt, xin_name, xout_name):
        self.epilogue_tiles([(tt, o_parts)], x_in_ap, x_out_ap,
                            xin_name, xout_name)

    def attn_pairs(self, cfg):
        p, t = self.p, self.t
        ps_s, ps_pv, ps_u = cfg["ps_s"], cfg["ps_pv"], cfg["ps_u"]
        pbuf, qbuf = cfg["pbuf"], cfg["qbuf"]
        recrow, recrow_bf = cfg["recrow"], cfg["recrow_bf"]
        recbc, ybst = cfg["recbc"], cfg["ybst"]
        yfm = cfg["yfm"]
        LAG = cfg.get("lag", 3)
        npb = pbuf.shape[1]

        def qslot(pair, tc):
            return tc % 2

        def qproj_emit(pair, tc):
            par = qslot(pair, tc)
            for d in range(cfg["qkc"]):
                wap, wkey = cfg["qw"](pair, d)
                sap, skey = cfg["q_src"](d, tc)
                p.emit("pe", lambda h, wap=wap, sap=sap, d=d: h.matmul(
                    ps_u[0][:], wap, sap,
                    start=(d == 0), stop=(d == cfg["qkc"] - 1)),
                    reads=[wkey, skey], writes=[("ps_u", 0)])
            p.emit("dve", lambda h, par=par: h.tensor_copy(
                qbuf[:, par, :], ps_u[0][:]),
                reads=[("ps_u", 0)], writes=[("qbuf", par)])

        def epi_emit(pair, tc):
            t0 = tc * 512
            par = tc % 2
            for head in range(2):
                p.emit("act", lambda h, head=head: h.activation(
                    out=recrow, in_=ps_pv[head][64:65, :],
                    func=AF.Ln),
                    reads=[("ps_pv", head)], writes=[("recrow",)])
                p.emit("act", lambda h: h.activation(
                    out=recrow_bf, in_=recrow,
                    func=AF.Exp, scale=-1.0),
                    reads=[("recrow",)], writes=[("recrow_bf",)])
                p.emit("pe", lambda h: h.matmul(
                    ps_u[1][0:64, :], t["ones_b"][64:65, 0:64],
                    recrow_bf, start=True, stop=True),
                    reads=[("recrow_bf",), ("ones_b",)], writes=[("ps_u", 1)])
                p.emit("dve", lambda h: h.tensor_copy(
                    recbc, ps_u[1][0:64, :]),
                    reads=[("ps_u", 1)], writes=[("recbc",)])
                if head == 0:
                    p.emit("dve", lambda h, pair=pair, t0=t0: h.tensor_tensor(
                        out=yfm[0:64, pair, t0:t0 + 512],
                        in0=ps_pv[0][0:64, :], in1=recbc, op=ALU.mult),
                        reads=[("ps_pv", 0), ("recbc",)],
                        writes=[("yfm", pair, tc, 0)])
                else:
                    p.emit("dve", lambda h: h.tensor_tensor(
                        out=ybst, in0=ps_pv[1][0:64, :],
                        in1=recbc, op=ALU.mult),
                        reads=[("ps_pv", 1), ("recbc",)],
                        writes=[("ybst", 0)])
                    p.emit("pool", lambda h, pair=pair, t0=t0:
                           h.dma_start(out=yfm[64:128, pair, t0:t0 + 512],
                                       in_=ybst),
                           reads=[("ybst", 0)],
                           writes=[("yfm", pair, tc, 1)], dma="yb0")

        def pv_emit(pair, tc, si, smax):
            pslot = si % npb
            for head in range(2):
                vap, vkey = cfg["v_ap"](pair, head, si)
                p.emit("pe", lambda h, vap=vap, head=head, si=si,
                       pslot=pslot, smax=smax, hb=head * 512: h.matmul(
                           ps_pv[head][:], vap,
                           pbuf[:, pslot, hb:hb + 512],
                           start=(si == 0), stop=(si == smax - 1)),
                       reads=[vkey, cfg["vones_key"], ("pbuf", pslot)],
                       writes=[("ps_pv", head)])

        pend = None
        for pair in range(NP):
            if cfg.get("prep_pair"):
                cfg["prep_pair"](pair)
            if pair == 0:
                qproj_emit(pair, 0)
            chunks, tail = [], []
            if cfg.get("prep_stream") and pair + 1 < NP:
                chunks = cfg["prep_stream"](pair + 1)
                tail = [lambda pair=pair: cfg["prep_tail"](pair + 1),
                        lambda pair=pair: qproj_emit(pair + 1, 0)]
            tot_si = sum((tc * 512 + 511) // 128 + 1 if cfg["causal"] else TN
                         for tc in range(TCN))
            every = max(1, tot_si // (len(chunks) + 1)) if chunks else 0
            gsi = 0
            for tc in range(TCN):
                t0 = tc * 512
                par = tc % 2
                qpar = qslot(pair, tc)
                smax = (t0 + 511) // 128 + 1 if cfg["causal"] else TN
                qnext = min(smax - 1, max(4, smax - 4))
                for si in range(smax):
                    s0 = si * 128
                    sp = si % 2
                    straddle = cfg["causal"] and (s0 + 127 > t0)
                    pslot = si % npb
                    for head in range(2):
                        hb = head * 64
                        kap, kkey = cfg["k_ap"](pair, head, si)
                        p.emit("pe", lambda h, kap=kap, hb=hb, sp=sp,
                               qpar=qpar:
                               h.matmul(ps_s[sp][:, hb * 8:hb * 8 + 512], kap,
                                        qbuf[hb:hb + 64, qpar, :],
                                        start=True, stop=True),
                               reads=[kkey, ("qbuf", qpar)],
                               writes=[("ps_s", sp, hb)])
                    p.emit("act", lambda h, sp=sp, pslot=pslot: h.activation(
                        out=pbuf[:, pslot, :], in_=ps_s[sp][:],
                        func=AF.Exp, scale=cfg["scale"]),
                        reads=[("ps_s", sp, 0), ("ps_s", sp, 64)],
                        writes=[("pbuf", pslot)])
                    if straddle:
                        p.emit("pool", lambda h, s0=s0, t0=t0, pslot=pslot:
                               h.affine_select(
                                   out=pbuf[:, pslot, :], in_=pbuf[:, pslot, :],
                                   pattern=[[0, 2], [1, 512]],
                                   compare_op=ALU.is_ge,
                                   fill=0.0, base=t0 - s0,
                                   channel_multiplier=-1),
                               reads=[("pbuf", pslot)], writes=[("pbuf", pslot)])
                    if si == 2 and pend is not None:
                        pend()
                        pend = None
                    gsi += 1
                    if chunks and every and gsi % every == 0:
                        chunks.pop(0)()
                    if tail and tc == TCN - 1 and si in (4, 12):
                        tail.pop(0)()
                    if si == qnext:
                        if tc + 1 < TCN:
                            qproj_emit(pair, tc + 1)
                        elif pair + 1 < NP and not cfg.get("prep_pair") \
                                and not cfg.get("prep_stream"):
                            qproj_emit(pair + 1, 0)
                    if si >= LAG:
                        pv_emit(pair, tc, si - LAG, smax)
                if pend is not None:
                    pend()
                    pend = None
                for si in range(max(0, smax - LAG), smax):
                    pv_emit(pair, tc, si, smax)
                pend = (lambda pair=pair, tc=tc: epi_emit(pair, tc))
            for f in chunks:
                f()
            for f in tail:
                f()
        if pend is not None:
            pend()

    def proj_epi(self, yfm, wproj, wpkey, bff_row, x_in, x_out,
                 xin_name, xout_name, banks_all):
        p, t = self.p, self.t
        banks = [banks_all[i:i + 2] for i in range(0, len(banks_all), 2)]
        nb = len(banks)
        for tp in range(0, TN, 2):
            pair = []
            for tt in (tp, tp + 1):
                parts = []
                for nh in range(2):
                    bk, bkey = banks[tt % nb][nh]
                    for j in range(DC):
                        rd = [("yfm", j, tt // 4, 0), ("yfm", j, tt // 4, 1)]
                        p.emit("pe", lambda h, j=j, nh=nh, tt=tt, bk=bk: h.matmul(
                            bk, yfm[:, j, tt * 128:(tt + 1) * 128],
                            wproj[:, j, nh * 512:(nh + 1) * 512],
                            start=(j == 0),
                            stop=(j == DC - 1 and bff_row is None)),
                            reads=[wpkey] + rd, writes=[bkey])
                    if bff_row is not None:
                        p.emit("pe", lambda h, nh=nh, bk=bk: h.matmul(
                            bk, t["ones_b"][0:1, 0:128],
                            bff_row[0:1, nh * 512:(nh + 1) * 512],
                            start=False, stop=True),
                            reads=[("bff",), ("ones_b",)], writes=[bkey])
                    parts.append((bk, 512, bkey))
                pair.append((tt, parts))
            self.epilogue_tiles(pair, x_in, x_out, xin_name, xout_name)

    def phase_a0(self, x_in, x_out, w, dbg=None):
        nc, p, t = self.nc, self.p, self.t
        with ExitStack() as ph:
            P = ph.enter_context
            ph2 = ExitStack()
            P2 = ph2.enter_context
            lat = P2(nc.sbuf_tensor(self._u("lat"), [128, LC, T], BF16))
            ufm0 = P2(nc.sbuf_tensor(self._u("ufm0"), [128, 2, DC, 512], BF16))
            k_sh = P2(nc.sbuf_tensor(self._u("k_sh"), [128, T], BF16))
            v_ext = P2(nc.sbuf_tensor(self._u("v_ext"), [128, TN, 65], BF16))
            qbuf = P2(nc.sbuf_tensor(self._u("qbuf"), [128, 2, 512], BF16))
            pbuf = P2(nc.sbuf_tensor(self._u("pbuf"), [128, 4, 1024], BF16))
            recrow = P2(nc.sbuf_tensor(self._u("recrow"), [128, 512], F32))
            recrow_bf = P2(nc.sbuf_tensor(self._u("recrow_bf"), [128, 512], BF16))
            recbc = P2(nc.sbuf_tensor(self._u("recbc"), [64, 512], F32))
            ybst = P2(nc.sbuf_tensor(self._u("ybst"), [64, 2, 512], BF16))
            wcd = P2(nc.sbuf_tensor(self._u("wcd"), [128, DC, 512], BF16))
            wqu = P2(nc.sbuf_tensor(self._u("wqu"), [128, LC, D], BF16))
            wkv = P2(nc.sbuf_tensor(self._u("wkv"), [128, LC, 128], BF16))

            self.trash = lambda w: t["u_fm"][:, 0, 0:w]
            p.begin()
            with ExitStack() as sg:
                S = sg.enter_context
                ps_tr = [S(nc.psum_tensor(self._u("ps_tr0"), [128, 128], BF16)),
                         S(nc.psum_tensor(self._u("ps_tr1"), [128, 128], BF16))]
                ps_w = [S(nc.psum_tensor(self._u("ps_w0"), [128, 512], F32)),
                        S(nc.psum_tensor(self._u("ps_w1"), [128, 512], F32))]
                for nm in ("wcd", "wqu", "wkv"):
                    dst = {"wcd": wcd, "wqu": wqu, "wkv": wkv}[nm]
                    p.emit("pool", lambda h, dst=dst, nm=nm: h.dma_start(
                        out=dst[:],
                        in_=w[nm].rearrange("(c p) j -> p c j", p=128)),
                        writes=[(nm,)], dma=nm)
                p.emit("dve", lambda h: h.memset(v_ext[:, :, 64:65], 1.0),
                       writes=[("v_ones",)])
                def a0_ln(tc):
                    par = tc % 2
                    def dest(i, j, par=par):
                        return (ufm0[:, par, j, i * 128:(i + 1) * 128],
                                ("ufm0", par, j))
                    self.ln_group(x_in, tc * 4, 4, dest, ps_tr, "x_in")

                a0_ln(0)
                for tc in range(TCN):
                    par = tc % 2
                    if tc + 1 < TCN:
                        a0_ln(tc + 1)
                    for lc in range(LC):
                        pw = ps_w[lc % 2]
                        for d in range(DC):
                            p.emit("pe", lambda h, lc=lc, d=d, pw=pw, par=par:
                                   h.matmul(pw[:],
                                            wcd[:, d, lc * 128:(lc + 1) * 128],
                                            ufm0[:, par, d, :],
                                            start=(d == 0), stop=(d == DC - 1)),
                                   reads=[("wcd",), ("ufm0", par, d)],
                                   writes=[("ps_w", lc % 2)])
                        p.emit("dve", lambda h, lc=lc, tc=tc, pw=pw:
                               h.tensor_copy(
                                   lat[:, lc, tc * 512:(tc + 1) * 512], pw[:]),
                               reads=[("ps_w", lc % 2)], writes=[("lat", lc, tc)])
                    for lc in range(LC):
                        p.emit("pe", lambda h, lc=lc, tc=tc: h.matmul(
                            ps_w[0][0:64, :], wkv[:, lc, 0:64],
                            lat[:, lc, tc * 512:(tc + 1) * 512],
                            start=(lc == 0), stop=(lc == LC - 1)),
                            reads=[("wkv",), ("lat", lc, tc)],
                            writes=[("ps_w", 0)])
                    p.emit("dve", lambda h, tc=tc: h.tensor_copy(
                        k_sh[0:64, tc * 512:(tc + 1) * 512], ps_w[0][0:64, :]),
                        reads=[("ps_w", 0)], writes=[("k_lo", tc)])
                    for i in range(4):
                        tt = tc * 4 + i
                        for lc in range(LC):
                            p.emit("pe", lambda h, lc=lc, tc=tc, tt=tt: h.matmul(
                                ps_w[1][:, 0:64],
                                lat[:, lc, tt * 128:(tt + 1) * 128],
                                wkv[:, lc, 64:128],
                                start=(lc == 0), stop=(lc == LC - 1)),
                                reads=[("wkv",), ("lat", lc, tc)],
                                writes=[("ps_w", 1)])
                        p.emit("dve", lambda h, tt=tt: h.tensor_copy(
                            v_ext[:, tt, 0:64], ps_w[1][:, 0:64]),
                            reads=[("ps_w", 1)], writes=[("v_ext", tt)])
                p.emit("pool", lambda h: h.dma_start(
                    out=k_sh[64:128, :], in_=k_sh[0:64, :]),
                    reads=[("k_lo", tc) for tc in range(TCN)],
                    writes=[("k_hi",)], dma="kdup")
                if dbg:
                    dbg(1, locals())
            p.end()
            if DEBUG_A0_SEGS < 2:
                return

            p.begin()
            with ExitStack() as sg:
                S = sg.enter_context
                ps_s = [S(nc.psum_tensor(self._u("ps_sP0"), [128, 1024], F32)),
                        S(nc.psum_tensor(self._u("ps_sP1"), [128, 1024], F32))]
                ps_pv = [S(nc.psum_tensor(self._u("ps_pvA"), [65, 512], F32)),
                         S(nc.psum_tensor(self._u("ps_pvB"), [65, 512], F32))]
                ps_u = [S(nc.psum_tensor(self._u("ps_u0"), [128, 512], F32)),
                        S(nc.psum_tensor(self._u("ps_u1"), [128, 512], F32))]
                yfm = t["u_fm"]
                cfg = dict(
                    causal=True, scale=0.125,
                    k_ap=lambda pair, head, si: (
                        k_sh[head * 64:head * 64 + 64,
                             si * 128:(si + 1) * 128],
                        ("k_lo", si // 4) if head == 0 else ("k_hi",)),
                    v_ap=lambda pair, head, si: (v_ext[:, si, :], ("v_ext", si)),
                    vones_key=("v_ones",),
                    qw=lambda pair, d: (
                        wqu[:, d, pair * 128:(pair + 1) * 128], ("wqu",)),
                    q_src=lambda d, tc: (
                        lat[:, d, tc * 512:(tc + 1) * 512], ("lat", d, tc)),
                    qkc=LC, yfm=yfm, ps_s=ps_s, ps_pv=ps_pv, ps_u=ps_u,
                    pbuf=pbuf, qbuf=qbuf, recrow=recrow[64:65, :],
                    recrow_bf=recrow_bf[64:65, :],
                    recbc=recbc[:], ybst=ybst[:, 0, :], prep_pair=None)
                self.attn_pairs(cfg)
                if dbg:
                    dbg(2, locals())
            p.end()
            ph2.close()
            if DEBUG_A0_SEGS < 3:
                return

            p.begin()
            with ExitStack() as sg:
                S = sg.enter_context
                ps_x = [S(nc.psum_tensor(self._u("ps_x%d" % i), [128, 512],
                                         F32)) for i in range(8)]
                wcp = S(nc.sbuf_tensor(self._u("wcp"), [128, DC, D], BF16))
                th0 = S(nc.sbuf_tensor(self._u("trash0"), [128, 1024], BF16))
                self.trash = lambda w, th0=th0: th0[:, 0:w]
                p.emit("pool", lambda h: h.dma_start(
                    out=wcp[:],
                    in_=w["wcp"].rearrange("(c p) j -> p c j", p=128)),
                    writes=[("wcp",)], dma="wcp")
                yfm = t["u_fm"]
                banks4 = [(ps_x[i][:], ("ps_x", i)) for i in range(8)]
                self.proj_epi(yfm, wcp, ("wcp",), None, x_in, x_out,
                              "x_in", "x_out", banks4)
            p.end()

    def phase_a1(self, x_in, x_out, w):
        nc, p, t = self.nc, self.p, self.t
        with ExitStack() as ph:
            P = ph.enter_context
            yfm = P(nc.sbuf_tensor(self._u("yfm"), [128, DC, T], BF16))
            ph2 = ExitStack()
            P2 = ph2.enter_context
            kpair = P2(nc.sbuf_tensor(self._u("kpair"), [128, T], BF16))
            v_ext = P2(nc.sbuf_tensor(self._u("v_ext1"), [128, 2, TN, 130], BF16))
            qbuf = P2(nc.sbuf_tensor(self._u("qbuf1"), [128, 2, 512], BF16))
            pbuf = P2(nc.sbuf_tensor(self._u("pbuf1"), [128, 3, 1024], BF16))
            epib = P2(nc.sbuf_tensor(self._u("epib1"), [128, 1024], BF16))
            recrow = epib[64:65, 0:512]
            recrow_bf = epib[64:65, 512:1024]
            recbc = epib[0:64, 0:512]
            ybst = epib[0:64, 512:1024]
            wq_p = P2(nc.sbuf_tensor(self._u("wq_p"), [128, DC, 128], BF16))
            wkv_p = P2(nc.sbuf_tensor(self._u("wkv_p"), [128, DC, 128], BF16))
            wk_p = wv_p = wkv_p

            self.trash = lambda w: yfm[:, 0, 0:w]
            p.begin()
            with ExitStack() as sg:
                S = sg.enter_context
                ps_tr = [S(nc.psum_tensor(self._u("ps_tr0"), [128, 128], BF16)),
                         S(nc.psum_tensor(self._u("ps_tr1"), [128, 128], BF16))]
                ufm = t["u_fm"]
                for tg in range(TN // 4):
                    def dest(i, j, tg=tg):
                        tt = tg * 4 + i
                        return (ufm[:, j, tt * 128:(tt + 1) * 128],
                                ("ufm", j, tt // 4))
                    self.ln_group(x_in, tg * 4, 4, dest, ps_tr, "x_in")
                p.emit("dve", lambda h: h.memset(v_ext[:, :, :, 64:65], 1.0),
                       writes=[("v_ones",)])
                p.emit("dve", lambda h: h.memset(v_ext[:, :, :, 129:130], 1.0),
                       writes=[("v_ones",)])
            p.end()

            p.begin()
            with ExitStack() as sg:
                S = sg.enter_context
                ps_s = [S(nc.psum_tensor(self._u("ps_sP0"), [128, 1024], F32)),
                        S(nc.psum_tensor(self._u("ps_sP1"), [128, 1024], F32))]
                ps_pv = [S(nc.psum_tensor(self._u("ps_pvA"), [65, 512], F32)),
                         S(nc.psum_tensor(self._u("ps_pvB"), [65, 512], F32))]
                ps_u = [S(nc.psum_tensor(self._u("ps_u0"), [128, 512], F32)),
                        S(nc.psum_tensor(self._u("ps_u1"), [128, 512], F32))]
                ufm = t["u_fm"]

                def wslice_load(nm, dst, pair):
                    p.emit("pool", lambda h, nm=nm, dst=dst, pair=pair:
                           h.dma_start(
                               out=dst[:],
                               in_=w[nm].rearrange("(c p) j -> p c j",
                                                   p=128)[
                                   :, :, pair * 128:(pair + 1) * 128]),
                           writes=[("wkv_p",) if nm != "wq" else ("wq_p",)], dma=nm + "p")

                def kproj_tc(pair, tc):
                    for d in range(DC):
                        p.emit("pe", lambda h, d=d, tc=tc: h.matmul(
                            ps_u[0][:], wk_p[:, d, :],
                            ufm[:, d, tc * 512:(tc + 1) * 512],
                            start=(d == 0), stop=(d == DC - 1)),
                            reads=[("wkv_p",), ("ufm", d, tc)],
                            writes=[("ps_u", 0)])
                    p.emit("dve", lambda h, tc=tc: h.tensor_copy(
                        kpair[:, tc * 512:(tc + 1) * 512], ps_u[0][:]),
                        reads=[("ps_u", 0)], writes=[("kpair", tc)])

                def vproj_tt(pair, tt):
                    vs = pair % 2
                    for d in range(DC):
                        p.emit("pe", lambda h, d=d, tt=tt: h.matmul(
                            ps_u[1][:, 0:128],
                            ufm[:, d, tt * 128:(tt + 1) * 128],
                            wv_p[:, d, :],
                            start=(d == 0), stop=(d == DC - 1)),
                            reads=[("wkv_p",), ("ufm", d, tt // 4)],
                            writes=[("ps_u", 1)])
                    p.emit("dve", lambda h, tt=tt, vs=vs: h.tensor_copy(
                        v_ext[:, vs, tt, 0:64], ps_u[1][:, 0:64]),
                        reads=[("ps_u", 1)], writes=[("v_ext", vs, tt)])
                    p.emit("dve", lambda h, tt=tt, vs=vs: h.tensor_copy(
                        v_ext[:, vs, tt, 65:129], ps_u[1][:, 64:128]),
                        reads=[("ps_u", 1)], writes=[("v_ext", vs, tt)])

                def prep_pair(pair):
                    if pair == 0:
                        wslice_load("wq", wq_p, pair)
                    wslice_load("wk", wk_p, pair)
                    for tc in range(TCN):
                        kproj_tc(pair, tc)
                    if pair == 0:
                        wslice_load("wv", wv_p, pair)
                        for tt in range(TN):
                            vproj_tt(pair, tt)

                def prep_stream(pair):
                    ch = [lambda: wslice_load("wv", wv_p, pair)]
                    ch += [lambda tt=tt: vproj_tt(pair, tt)
                           for tt in range(TN)]
                    return ch

                cfg = dict(
                    causal=False, scale=1.0 / 32.0, lag=2,
                    k_ap=lambda pair, head, si: (
                        kpair[head * 64:head * 64 + 64,
                              si * 128:(si + 1) * 128],
                        ("kpair", si // 4)),
                    v_ap=lambda pair, head, si: (
                        v_ext[:, pair % 2, si, head * 65:head * 65 + 65],
                        ("v_ext", pair % 2, si)),
                    vones_key=("v_ones",),
                    qw=lambda pair, d: (wq_p[:, d, :], ("wq_p",)),
                    q_src=lambda d, tc: (
                        ufm[:, d, tc * 512:(tc + 1) * 512], ("ufm", d, tc)),
                    qkc=DC, yfm=yfm, ps_s=ps_s, ps_pv=ps_pv, ps_u=ps_u,
                    pbuf=pbuf, qbuf=qbuf, recrow=recrow, recrow_bf=recrow_bf,
                    recbc=recbc, ybst=ybst, prep_pair=prep_pair,
                    prep_stream=prep_stream,
                    prep_tail=lambda pair: wslice_load("wq", wq_p, pair))
                self.attn_pairs(cfg)
            p.end()
            ph2.close()

            self.trash = lambda w: t["u_fm"][:, 0, 0:w]
            p.begin()
            with ExitStack() as sg:
                S = sg.enter_context
                ps_x = [S(nc.psum_tensor(self._u("ps_x%d" % i), [128, 512],
                                             F32)) for i in range(8)]
                wff = S(nc.sbuf_tensor(self._u("wff"), [128, DC, D], BF16))
                bffr = S(nc.sbuf_tensor(self._u("bffr"), [1, D], BF16))
                p.emit("pool", lambda h: h.dma_start(
                    out=bffr[:], in_=w["bff"][:, :]),
                    writes=[("bff",)], dma="bff")
                p.emit("pool", lambda h: h.dma_start(
                    out=wff[:],
                    in_=w["wff"].rearrange("(c p) j -> p c j", p=128)),
                    writes=[("wff",)], dma="wffl")
                banks4 = [(ps_x[i][:], ("ps_x", i)) for i in range(8)]
                self.proj_epi(yfm, wff, ("wff",), bffr, x_in, x_out,
                              "x_in", "x_out", banks4)
            p.end()

    def phase_f0(self, x_in, x_out, w):
        nc, p, t = self.nc, self.p, self.t
        with ExitStack() as ph:
            P = ph.enter_context
            acc = P(nc.sbuf_tensor(self._u("acc"), [128, DC, D], F32))
            act_f = P(nc.sbuf_tensor(self._u("act_f"), [128, DC, 1024], BF16))
            sg_b = P(nc.sbuf_tensor(self._u("sg_b"), [128, 2, 512], BF16))
            cw_sb = P(nc.sbuf_tensor(self._u("cw_sb"), [128, 2, 8, 8], F32))
            prob = P(nc.sbuf_tensor(self._u("prob"), [128, 2, 8], F32))
            gsml = P(nc.sbuf_tensor(self._u("gsml"), [128, 2, 8], F32))
            gst = P(nc.sbuf_tensor(self._u("gst"), [128, 24], F32))
            gw_sb = P(nc.sbuf_tensor(self._u("gw_sb"), [128, DC, 8], BF16))
            wg = P(nc.sbuf_tensor(self._u("wg"), [128, DC, IM], BF16))
            wu = P(nc.sbuf_tensor(self._u("wu"), [128, DC, IM], BF16))
            wd = P(nc.sbuf_tensor(self._u("wd"), [128, DC, D], BF16))
            thf0 = P(nc.sbuf_tensor(self._u("thf0"), [128, 1024], BF16))
            self.trash = lambda w, thf0=thf0: thf0[:, 0:w]

            p.begin()
            with ExitStack() as sg:
                S = sg.enter_context
                ps_tr = [S(nc.psum_tensor(self._u("ps_tr0"), [128, 128], BF16)),
                         S(nc.psum_tensor(self._u("ps_tr1"), [128, 128], BF16))]
                ps_h = [S(nc.psum_tensor(self._u("ps_h0"), [128, 512], F32)),
                        S(nc.psum_tensor(self._u("ps_h1"), [128, 512], F32))]
                ps_d = [S(nc.psum_tensor(self._u("ps_d0"), [128, 512], F32)),
                        S(nc.psum_tensor(self._u("ps_d1"), [128, 512], F32))]
                ps_u = [S(nc.psum_tensor(self._u("ps_u0"), [128, 512], F32)),
                        S(nc.psum_tensor(self._u("ps_u1"), [128, 512], F32))]
                ufm = t["u_fm"]
                st = gst
                p.emit("pool", lambda h: h.dma_start(
                    out=gw_sb[:],
                    in_=w["gw"].rearrange("(c p) e -> p c e", p=128)),
                    writes=[("gw",)], dma="gwl")

                def wload(nm, dst, e, cols, grp):
                    src = w[nm][e]
                    K = src.shape[0]
                    if K % 128 == 0:
                        p.emit("pool", lambda h: h.dma_start(
                            out=dst[:, 0:K // 128, 0:cols],
                            in_=src.rearrange("(c p) j -> p c j", p=128)),
                            writes=[(grp,)], dma=grp)
                    else:
                        full = (K // 128) * 128
                        p.emit("pool", lambda h: h.dma_start(
                            out=dst[:, 0:full // 128, 0:cols],
                            in_=src[0:full].rearrange("(c p) j -> p c j", p=128)),
                            writes=[(grp,)], dma=grp)
                        p.emit("pool", lambda h: h.dma_start(
                            out=dst[0:K - full, full // 128, 0:cols],
                            in_=src[full:K]),
                            writes=[(grp,)], dma=grp)

                def ln_four(tt0):
                    def dest(i, j, tt0=tt0):
                        tt = tt0 + i
                        return (ufm[:, j, tt * 128:(tt + 1) * 128],
                                ("ufm", j, tt // 4))
                    self.ln_group(x_in, tt0, 4, dest, ps_tr, "x_in")

                def ln_chunk(tci):
                    ln_four(tci * 8)
                    ln_four(tci * 8 + 4)

                def gate_chunk(tci):
                    for i in range(8):
                        gate_one(tci, i)

                def gate_one(tci, i):
                    cpar = tci % 2
                    if True:
                        tt = tci * 8 + i
                        gpar = i % 2
                        for j in range(DC):
                            p.emit("pe", lambda h, j=j, tt=tt: h.matmul(
                                ps_u[0][:, 0:8],
                                ufm[:, j, tt * 128:(tt + 1) * 128],
                                gw_sb[:, j, :],
                                start=(j == 0), stop=(j == DC - 1)),
                                reads=[("gw",), ("ufm", j, tt // 4)],
                                writes=[("ps_u", 0)])
                        ko = ("cwt", gpar)
                        p.emit("act", lambda h, gpar=gpar: h.activation(
                            out=prob[:, gpar, :], in_=ps_u[0][:, 0:8],
                            func=AF.Exp, accum_out=st[:, 8 + gpar:9 + gpar]),
                            reads=[("ps_u", 0)], writes=[ko])
                        p.emit("dve", lambda h, gpar=gpar: h.reciprocal(
                            st[:, 10 + gpar:11 + gpar],
                            st[:, 8 + gpar:9 + gpar]),
                            reads=[ko], writes=[ko])
                        p.emit("dve", lambda h, gpar=gpar: h.tensor_scalar(
                            out=prob[:, gpar, :], in0=prob[:, gpar, :],
                            scalar1=st[:, 10 + gpar:11 + gpar], scalar2=None,
                            op0=ALU.mult, op1=ALU.bypass),
                            reads=[ko], writes=[ko])
                        p.emit("dve", lambda h, gpar=gpar: h.reduce_max(
                            st[:, 12 + gpar:13 + gpar], prob[:, gpar, :],
                            axis=mybir.AxisListType.X),
                            reads=[ko], writes=[ko])
                        p.emit("dve", lambda h, gpar=gpar: h.tensor_scalar(
                            out=gsml[:, gpar, :], in0=prob[:, gpar, :],
                            scalar1=st[:, 12 + gpar:13 + gpar], scalar2=None,
                            op0=ALU.is_equal, op1=ALU.bypass),
                            reads=[ko], writes=[ko])
                        p.emit("dve", lambda h, gpar=gpar: h.scalar_tensor_tensor(
                            out=gsml[:, gpar, :], in0=gsml[:, gpar, :],
                            scalar=-2.0, in1=prob[:, gpar, :],
                            op0=ALU.mult, op1=ALU.add),
                            reads=[ko], writes=[ko])
                        p.emit("dve", lambda h, gpar=gpar: h.reduce_max(
                            st[:, 14 + gpar:15 + gpar], gsml[:, gpar, :],
                            axis=mybir.AxisListType.X),
                            reads=[ko], writes=[ko])
                        p.emit("dve", lambda h, gpar=gpar: h.tensor_tensor(
                            out=st[:, 16 + gpar:17 + gpar],
                            in0=st[:, 12 + gpar:13 + gpar],
                            in1=st[:, 14 + gpar:15 + gpar], op=ALU.add),
                            reads=[ko], writes=[ko])
                        p.emit("dve", lambda h, gpar=gpar: h.tensor_scalar(
                            out=st[:, 16 + gpar:17 + gpar],
                            in0=st[:, 16 + gpar:17 + gpar],
                            scalar1=1e-8, scalar2=None,
                            op0=ALU.add, op1=ALU.bypass),
                            reads=[ko], writes=[ko])
                        p.emit("dve", lambda h, gpar=gpar: h.reciprocal(
                            st[:, 18 + gpar:19 + gpar],
                            st[:, 16 + gpar:17 + gpar]),
                            reads=[ko], writes=[ko])
                        p.emit("dve", lambda h, gpar=gpar: h.tensor_scalar(
                            out=gsml[:, gpar, :], in0=prob[:, gpar, :],
                            scalar1=st[:, 14 + gpar:15 + gpar], scalar2=None,
                            op0=ALU.is_ge, op1=ALU.bypass),
                            reads=[ko], writes=[ko])
                        p.emit("dve", lambda h, gpar=gpar: h.tensor_tensor(
                            out=gsml[:, gpar, :], in0=gsml[:, gpar, :],
                            in1=prob[:, gpar, :], op=ALU.mult),
                            reads=[ko], writes=[ko])
                        p.emit("dve", lambda h, gpar=gpar, i=i, cpar=cpar:
                               h.tensor_scalar(
                                   out=cw_sb[:, cpar, i, :],
                                   in0=gsml[:, gpar, :],
                                   scalar1=st[:, 18 + gpar:19 + gpar],
                                   scalar2=None,
                                   op0=ALU.mult, op1=ALU.bypass),
                               reads=[ko], writes=[("cw", cpar, i)])

                def epi_pair(tci, i):
                    pair = []
                    for k in (i, i + 1):
                        tt = tci * 8 + k
                        pair.append((tt, [(acc[:, k, 0:512], 512, ("acc", k, 0)),
                                          (acc[:, k, 512:1024], 512,
                                           ("acc", k, 1))]))
                    self.epilogue_tiles(pair, x_in, x_out, "x_in", "x_out")

                def epi_chunk(tci):
                    for i in range(0, 8, 2):
                        epi_pair(tci, i)

                def experts(tci, inj):
                    cpar = tci % 2
                    for e in range(9):
                        wload("gug", wg, e, IM, "wg")
                        wload("guu", wu, e, IM, "wu")
                        wload("dwn", wd, e, D, "wd")
                        for th in range(2):
                            for f in inj.get((e, th), ()):
                                f()
                            tc = tci * 2 + th
                            for fc in range(DC):
                                fw = ICW[fc]
                                for d in range(DC):
                                    p.emit("pe", lambda h, fc=fc, fw=fw, d=d,
                                           tc=tc: h.matmul(
                                               ps_h[0][0:fw, :],
                                               wg[:, d, fc * 128:fc * 128 + fw],
                                               ufm[:, d,
                                                   tc * 512:(tc + 1) * 512],
                                               start=(d == 0),
                                               stop=(d == DC - 1)),
                                           reads=[("wg",), ("ufm", d, tc)],
                                           writes=[("ps_h", 0)])
                                for d in range(DC):
                                    p.emit("pe", lambda h, fc=fc, fw=fw, d=d,
                                           tc=tc: h.matmul(
                                               ps_h[1][0:fw, :],
                                               wu[:, d, fc * 128:fc * 128 + fw],
                                               ufm[:, d,
                                                   tc * 512:(tc + 1) * 512],
                                               start=(d == 0),
                                               stop=(d == DC - 1)),
                                           reads=[("wu",), ("ufm", d, tc)],
                                           writes=[("ps_h", 1)])
                                spar = fc % 2
                                p.emit("act", lambda h, fw=fw, spar=spar:
                                       h.activation(
                                           out=sg_b[0:fw, spar, :],
                                           in_=ps_h[0][0:fw, :], func=AF.Silu),
                                       reads=[("ps_h", 0)],
                                       writes=[("sg", spar)])
                                p.emit("dve", lambda h, fw=fw, fc=fc, th=th,
                                       spar=spar: h.tensor_tensor(
                                           out=act_f[0:fw, fc,
                                                     th * 512:(th + 1) * 512],
                                           in0=ps_h[1][0:fw, :],
                                           in1=sg_b[0:fw, spar, :],
                                           op=ALU.mult),
                                       reads=[("ps_h", 1), ("sg", spar)],
                                       writes=[("act_f", fc)])
                        for i in range(8):
                            for f in inj.get((e, "dn", i), ()):
                                f()
                            for nh in range(2):
                                dpar = (i * 2 + nh) % 2
                                for ic in range(DC):
                                    iw = ICW[ic]
                                    p.emit("pe", lambda h, ic=ic, iw=iw, i=i,
                                           nh=nh, dpar=dpar: h.matmul(
                                               ps_d[dpar][:],
                                               act_f[0:iw, ic,
                                                     i * 128:(i + 1) * 128],
                                               wd[0:iw, ic,
                                                  nh * 512:(nh + 1) * 512],
                                               start=(ic == 0),
                                               stop=(ic == DC - 1)),
                                           reads=[("wd",), ("act_f", ic)],
                                           writes=[("ps_d", dpar)])
                                if e == 0:
                                    p.emit("dve", lambda h, i=i, nh=nh,
                                           dpar=dpar: h.tensor_copy(
                                               acc[:, i,
                                                   nh * 512:(nh + 1) * 512],
                                               ps_d[dpar][:]),
                                           reads=[("ps_d", dpar)],
                                           writes=[("acc", i, nh)])
                                else:
                                    p.emit("dve", lambda h, i=i, nh=nh, e=e,
                                           dpar=dpar, cpar=cpar:
                                           h.scalar_tensor_tensor(
                                               out=acc[:, i,
                                                       nh * 512:(nh + 1) * 512],
                                               in0=ps_d[dpar][:],
                                               scalar=cw_sb[:, cpar, i, e - 1:e],
                                               in1=acc[:, i,
                                                       nh * 512:(nh + 1) * 512],
                                               op0=ALU.mult, op1=ALU.add),
                                           reads=[("ps_d", dpar),
                                                  ("cw", cpar, i),
                                                  ("acc", i, nh)],
                                           writes=[("acc", i, nh)])

                ln_chunk(0)
                gate_chunk(0)
                slots = [(e, th) for e in range(9) for th in range(2)]
                for tci in range(4):
                    inj = {}
                    def add(slot, f):
                        inj.setdefault(slot, []).append(f)
                    pass
                    if tci < 3:
                        add(slots[8], lambda tci=tci: ln_four((tci + 1) * 8))
                        add(slots[11], lambda tci=tci: ln_four((tci + 1) * 8 + 4))
                        for i in range(8):
                            add(slots[10 + i] if 10 + i < 18 else slots[17],
                                lambda tci=tci, i=i: gate_one(tci + 1, i))
                    experts(tci, inj)
                    epi_chunk(tci)
            p.end()

    def phase_f1(self, x_in, x_out, w):
        nc, p, t = self.nc, self.p, self.t
        with ExitStack() as ph:
            P = ph.enter_context
            xr = P(nc.sbuf_tensor(self._u("xr"), [128, DC, 512], BF16))
            xk = P(nc.sbuf_tensor(self._u("xk"), [128, DC, 512], BF16))
            ka = P(nc.sbuf_tensor(self._u("ka"), [128, DC, 512], BF16))
            r_sb = P(nc.sbuf_tensor(self._u("r_sb"), [128, 4, D], BF16))
            o_sb = P(nc.sbuf_tensor(self._u("o_sb"), [128, 2, D], F32))
            rk_t = P(nc.sbuf_tensor(self._u("rk_t"), [128, 2, 512], F32))
            dmix = P(nc.sbuf_tensor(self._u("dmix"), [128, 2, 512], BF16))
            mu_r = P(nc.sbuf_tensor(self._u("mu_r"), [128, DC], F32))
            mu_k = P(nc.sbuf_tensor(self._u("mu_k"), [128, DC], F32))
            om_r = P(nc.sbuf_tensor(self._u("om_r"), [128, DC], F32))
            om_k = P(nc.sbuf_tensor(self._u("om_k"), [128, DC], F32))
            wr = P(nc.sbuf_tensor(self._u("wr"), [128, DC, D], BF16))
            wkk = P(nc.sbuf_tensor(self._u("wkk"), [128, DC, D], BF16))
            wvv = P(nc.sbuf_tensor(self._u("wvv"), [128, DC, D], BF16))
            thf1 = P(nc.sbuf_tensor(self._u("thf1"), [128, 1024], BF16))
            self.trash = lambda w, thf1=thf1: thf1[:, 0:w]

            p.begin()
            with ExitStack() as sg:
                S = sg.enter_context
                ps_tr = [S(nc.psum_tensor(self._u("ps_tr0"), [128, 128], BF16)),
                         S(nc.psum_tensor(self._u("ps_tr1"), [128, 128], BF16))]
                ps_g = [S(nc.psum_tensor(self._u("ps_g%d" % i), [128, 512], F32))
                        for i in range(4)]
                ufm = t["u_fm"]
                for nm, dst in (("wr", wr), ("wkk", wkk), ("wvv", wvv)):
                    p.emit("pool", lambda h, nm=nm, dst=dst: h.dma_start(
                        out=dst[:],
                        in_=w[nm].rearrange("(c p) j -> p c j", p=128)),
                        writes=[(nm,)], dma=nm)
                for nm, dst in (("mur", mu_r), ("muk", mu_k)):
                    p.emit("sp", lambda h, nm=nm, dst=dst: h.dma_start(
                        out=dst[:], in_=w[nm][:, :]),
                        writes=[(nm,)], dma="mu")
                p.emit("dve", lambda h: h.tensor_scalar(
                    out=om_r[:], in0=mu_r[:], scalar1=-1.0, scalar2=1.0,
                    op0=ALU.mult, op1=ALU.add),
                    reads=[("mur",)], writes=[("omr",)])
                p.emit("dve", lambda h: h.tensor_scalar(
                    out=om_k[:], in0=mu_k[:], scalar1=-1.0, scalar2=1.0,
                    op0=ALU.mult, op1=ALU.add),
                    reads=[("muk",)], writes=[("omk",)])
                for tg in range(TN // 4):
                    def dest(i, j, tg=tg):
                        tt = tg * 4 + i
                        return (ufm[:, j, tt * 128:(tt + 1) * 128],
                                ("ufm", j, tt // 4))
                    self.ln_group(x_in, tg * 4, 4, dest, ps_tr, "x_in")
                def mixes(tc):
                    t0 = tc * 512
                    for cc in range(DC):
                        ds = cc % 2
                        if tc == 0:
                            p.emit("dve", lambda h, cc=cc, ds=ds:
                                   h.tensor_tensor(
                                       out=dmix[:, ds, 1:512],
                                       in0=ufm[:, cc, 1:512],
                                       in1=ufm[:, cc, 0:511], op=ALU.subtract),
                                   reads=[("ufm", cc, 0)],
                                   writes=[("dmix", ds)])
                            p.emit("dve", lambda h, cc=cc, ds=ds:
                                   h.tensor_copy(dmix[:, ds, 0:1],
                                                 ufm[:, cc, 0:1]),
                                   reads=[("ufm", cc, 0)], writes=[("dmix", ds)])
                        else:
                            p.emit("dve", lambda h, cc=cc, ds=ds, t0=t0:
                                   h.tensor_tensor(
                                       out=dmix[:, ds, :],
                                       in0=ufm[:, cc, t0:t0 + 512],
                                       in1=ufm[:, cc, t0 - 1:t0 + 511],
                                       op=ALU.subtract),
                                   reads=[("ufm", cc, tc), ("ufm", cc, tc - 1)],
                                   writes=[("dmix", ds)])
                        for mu, dst, dk in ((mu_r, xr, "xr"), (mu_k, xk, "xk")):
                            if tc == 0:
                                p.emit("dve", lambda h, cc=cc, ds=ds, mu=mu,
                                       dst=dst: h.scalar_tensor_tensor(
                                           out=dst[:, cc, 1:512],
                                           in0=dmix[:, ds, 1:512],
                                           scalar=mu[:, cc:cc + 1],
                                           in1=ufm[:, cc, 0:511],
                                           op0=ALU.mult, op1=ALU.add),
                                       reads=[("dmix", ds), ("ufm", cc, 0)],
                                       writes=[(dk, cc)])
                                p.emit("dve", lambda h, cc=cc, ds=ds, mu=mu,
                                       dst=dst: h.tensor_scalar(
                                           out=dst[:, cc, 0:1],
                                           in0=dmix[:, ds, 0:1],
                                           scalar1=mu[:, cc:cc + 1],
                                           scalar2=None, op0=ALU.mult,
                                           op1=ALU.bypass),
                                       reads=[("dmix", ds)], writes=[(dk, cc)])
                            else:
                                p.emit("dve", lambda h, cc=cc, ds=ds, mu=mu,
                                       dst=dst, t0=t0: h.scalar_tensor_tensor(
                                           out=dst[:, cc, :],
                                           in0=dmix[:, ds, :],
                                           scalar=mu[:, cc:cc + 1],
                                           in1=ufm[:, cc, t0 - 1:t0 + 511],
                                           op0=ALU.mult, op1=ALU.add),
                                       reads=[("dmix", ds), ("ufm", cc, tc),
                                              ("ufm", cc, tc - 1)],
                                       writes=[(dk, cc)])
                mixes(0)
                for tc in range(TCN):
                    t0 = tc * 512
                    self.epi_loads([tc * 4, tc * 4 + 1], x_in, "x_in")
                    for i in range(4):
                        for nh in range(2):
                            gp = (i * 2 + nh) % 2
                            for cc in range(DC):
                                p.emit("pe", lambda h, cc=cc, i=i, nh=nh, gp=gp:
                                       h.matmul(
                                           ps_g[gp][:],
                                           xr[:, cc, i * 128:(i + 1) * 128],
                                           wr[:, cc, nh * 512:(nh + 1) * 512],
                                           start=(cc == 0), stop=(cc == DC - 1)),
                                       reads=[("wr",), ("xr", cc)],
                                       writes=[("ps_g", gp)])
                            p.emit("act", lambda h, i=i, nh=nh, gp=gp:
                                   h.activation(
                                       out=r_sb[:, i, nh * 512:(nh + 1) * 512],
                                       in_=ps_g[gp][:], func=AF.Sigmoid),
                                   reads=[("ps_g", gp)], writes=[("r", i)])
                    for mc in range(DC):
                        gp = 2 + mc % 2
                        for cc in range(DC):
                            p.emit("pe", lambda h, cc=cc, mc=mc, gp=gp: h.matmul(
                                ps_g[gp][:],
                                wkk[:, cc, mc * 128:(mc + 1) * 128],
                                xk[:, cc, :],
                                start=(cc == 0), stop=(cc == DC - 1)),
                                reads=[("wkk",), ("xk", cc)],
                                writes=[("ps_g", gp)])
                        rp = mc % 2
                        p.emit("act", lambda h, gp=gp, rp=rp: h.activation(
                            out=rk_t[:, rp, :], in_=ps_g[gp][:], func=AF.Relu),
                            reads=[("ps_g", gp)], writes=[("rk_t", rp)])
                        p.emit("act", lambda h, mc=mc, rp=rp: h.activation(
                            out=ka[:, mc, :], in_=rk_t[:, rp, :],
                            func=AF.Square),
                            reads=[("rk_t", rp)], writes=[("ka", mc)])
                    if tc + 1 < TCN:
                        mixes(tc + 1)
                    for i in range(4):
                        tt = tc * 4 + i
                        osl = i % 2
                        for nh in range(2):
                            gp = nh
                            for cc in range(DC):
                                p.emit("pe", lambda h, cc=cc, i=i, nh=nh, gp=gp:
                                       h.matmul(
                                           ps_g[gp][:],
                                           ka[:, cc, i * 128:(i + 1) * 128],
                                           wvv[:, cc, nh * 512:(nh + 1) * 512],
                                           start=(cc == 0), stop=(cc == DC - 1)),
                                       reads=[("wvv",), ("ka", cc)],
                                       writes=[("ps_g", gp)])
                            p.emit("dve", lambda h, i=i, nh=nh, gp=gp, osl=osl:
                                   h.tensor_tensor(
                                       out=o_sb[:, osl, nh * 512:(nh + 1) * 512],
                                       in0=ps_g[gp][:],
                                       in1=r_sb[:, i, nh * 512:(nh + 1) * 512],
                                       op=ALU.mult),
                                   reads=[("ps_g", gp), ("r", i)],
                                   writes=[("o", osl)])
                        if i % 2 == 1:
                            self.epilogue_tiles(
                                [(tt - 1, [(o_sb[:, 0, :], D, ("o", 0))]),
                                 (tt, [(o_sb[:, 1, :], D, ("o", 1))])],
                                x_in, x_out, "x_in", "x_out",
                                skip_loads=True)
                            if i == 1:
                                self.epi_loads([tt + 1, tt + 2],
                                               x_in, "x_in")
            p.end()


PHASE_WEIGHTS = {
    "a0": ["wcd", "wqu", "wkv", "wcp"],
    "f0": ["gw", "gug", "guu", "dwn"],
    "a1": ["wq", "wk", "wv", "wff", "bff"],
    "f1": ["wr", "wkk", "wvv", "mur", "muk"],
}
WSHAPES = {
    "wcd": [D, 512], "wqu": [512, D], "wkv": [512, 128], "wcp": [D, D],
    "gw": [D, 8], "gug": [9, D, IM], "guu": [9, D, IM], "dwn": [9, IM, D],
    "wq": [D, D], "wk": [D, D], "wv": [D, D], "wff": [D, D], "bff": [1, D],
    "wr": [D, D], "wkk": [D, D], "wvv": [D, D], "mur": [128, 8], "muk": [128, 8],
}


def build_program(phases, first, last):
    nc = bass.Bass("TRN2", target_bir_lowering=False,
                   detect_race_conditions=False)
    names = {"a0": ("x0", "x1"), "f0": ("x1", "x2"),
             "a1": ("x2", "x3"), "f1": ("x3", "y")}
    dram = {}
    xio = {}
    for i, phn in enumerate(phases):
        nin, nout = names[phn]
        if nin not in xio:
            xio[nin] = nc.dram_tensor(nin, [T, D], F32,
                                      kind="ExternalInput").ap()
        if nout not in xio:
            kind = "ExternalOutput" if (phn == phases[-1]) else "Internal"
            xio[nout] = nc.dram_tensor(nout, [T, D], F32, kind=kind).ap()
        for wn in PHASE_WEIGHTS[phn]:
            if wn not in dram:
                dram[wn] = nc.dram_tensor(wn, WSHAPES[wn], F32,
                                          kind="ExternalInput").ap()
    with ExitStack() as es:
        p = Prog(nc, es)
        b = Build(nc, p, es)
        b.alloc_common()
        p.begin()
        b.setup_common()
        p.end()
        for phn in phases:
            nin, nout = names[phn]
            w = {k: dram[k] for k in PHASE_WEIGHTS[phn]}
            getattr(b, "phase_" + phn)(xio[nin], xio[nout], w)
        p.drain()
    return nc, [names[phn][1] for phn in phases][-1]


_PROG_CACHE = {}


def _host_weights(inputs):
    C = np.ascontiguousarray
    w = {}
    w["wcd"] = C(inputs["L0_attn_Wc_down"])
    w["wqu"] = C(inputs["L0_attn_Wq_up"])
    w["wkv"] = C(inputs["L0_attn_Wkv_up"])
    w["wcp"] = C(inputs["L0_attn_Wc_proj"])
    w["gw"] = C(inputs["L0_ffn_gate_weight"].T)
    sgu = inputs["L0_ffn_shared_gate_up"]
    egu = inputs["L0_ffn_experts_gate_up"]
    w["gug"] = C(np.stack([sgu[0:IM].T] + [egu[e][0:IM].T for e in range(8)]))
    w["guu"] = C(np.stack([sgu[IM:2 * IM].T] + [egu[e][IM:2 * IM].T
                                                for e in range(8)]))
    sdn = inputs["L0_ffn_shared_down"]
    edn = inputs["L0_ffn_experts_down"]
    w["dwn"] = C(np.stack([sdn.T] + [edn[e].T for e in range(8)]))
    w["wq"] = C(inputs["L1_attn_Wq"].T)
    w["wk"] = C(inputs["L1_attn_Wk"].T)
    w["wv"] = C(inputs["L1_attn_Wv"].T)
    w["wff"] = C(inputs["L1_attn_Wff"].T)
    w["bff"] = C(inputs["L1_attn_bff"].reshape(1, D))
    w["wr"] = C(inputs["L1_ffn_W_r"].T)
    w["wkk"] = C(inputs["L1_ffn_W_k"].T)
    w["wvv"] = C(inputs["L1_ffn_W_v"].T)
    w["mur"] = C(inputs["L1_ffn_mu_r"].reshape(8, 128).T)
    w["muk"] = C(inputs["L1_ffn_mu_k"].reshape(8, 128).T)
    return w


def kernel(**inputs):
    x = np.ascontiguousarray(inputs["L0_attn_x"].astype(np.float32))
    w = _host_weights(inputs)
    ncores = 8
    xcur = {"x0": [x[c] for c in range(ncores)]}
    out = None
    for li, phases in enumerate(LAUNCH_PLAN):
        key = tuple(phases)
        if key not in _PROG_CACHE:
            _PROG_CACHE[key] = build_program(phases, li == 0,
                                             li == len(LAUNCH_PLAN) - 1)
        nc, outname = _PROG_CACHE[key]
        names = {"a0": ("x0", "x1"), "f0": ("x1", "x2"),
                 "a1": ("x2", "x3"), "f1": ("x3", "y")}
        nin = names[phases[0]][0]
        in_maps = []
        for c in range(ncores):
            m = {nin: xcur[nin][c]}
            for phn in phases:
                for wn in PHASE_WEIGHTS[phn]:
                    m[wn] = w[wn]
            in_maps.append(m)
        res = run_bass_kernel_spmd(nc, in_maps, core_ids=list(range(ncores)))
        outs = [res.results[c][outname] for c in range(ncores)]
        xcur[outname] = outs
        out = outs
    return np.stack(out).astype(np.float32)
